# version 05
# baseline (speedup 1.0000x reference)
import numpy as np
import ml_dtypes
import concourse.bass as bass
import concourse.mybir as mybir
from concourse.bass_utils import run_bass_kernel_spmd
from concourse.library_config import ap_gather as _APG_LIB, standard as _STD_LIB

F32 = mybir.dt.float32
BF16 = mybir.dt.bfloat16
FP8 = mybir.dt.float8e5
I16 = mybir.dt.int16
Alu = mybir.AluOpType
AF = mybir.ActivationFunctionType
AX = mybir.AxisListType
NPBF16 = ml_dtypes.bfloat16

F = 1024
KC = 8
SEQ = 4096
WIN = 512
NT = 4096
HID = 4096
HC = 32
SLAB = 512
NHEAD = 8
NST = 32


def _threefry2x32(k1, k2, x1, x2):
    def rotl(v, d):
        return ((v << np.uint32(d)) | (v >> np.uint32(32 - d))).astype(np.uint32)
    ks = [np.uint32(k1), np.uint32(k2),
          np.uint32(k1) ^ np.uint32(k2) ^ np.uint32(0x1BD11BDA)]
    x = [(x1 + ks[0]).astype(np.uint32), (x2 + ks[1]).astype(np.uint32)]
    rot = [[13, 15, 26, 6], [17, 29, 16, 24]]
    inj = [(1, 2, 1), (2, 0, 2), (0, 1, 3), (1, 2, 4), (2, 0, 5)]
    for grp in range(5):
        for d in rot[grp % 2]:
            x[0] = (x[0] + x[1]).astype(np.uint32)
            x[1] = rotl(x[1], d)
            x[1] = x[0] ^ x[1]
        a, b, c = inj[grp]
        x[0] = (x[0] + ks[a]).astype(np.uint32)
        x[1] = (x[1] + ks[b] + np.uint32(c)).astype(np.uint32)
    return x[0], x[1]


def _dropout_mask():
    n = 8 * SEQ * F
    iota = np.arange(n, dtype=np.uint64)
    c1 = (iota >> np.uint64(32)).astype(np.uint32)
    c2 = (iota & np.uint64(0xFFFFFFFF)).astype(np.uint32)
    b1, b2 = _threefry2x32(0, 42, c1, c2)
    bits = (b1 ^ b2).astype(np.uint32)
    u = ((bits >> np.uint32(9)) | np.uint32(0x3F800000)).view(np.float32) - np.float32(1.0)
    return (u < np.float32(0.8)).reshape(8, SEQ, F)


def _interleave_idx(idx):
    n = len(idx)
    assert n % 16 == 0
    out = np.zeros((128, n // 16), np.int16)
    blk = np.asarray(idx, np.int64).reshape(n // 16, 16).T.astype(np.int16)
    for g in range(8):
        out[16 * g:16 * g + 16, :] = blk
    return np.ascontiguousarray(out)


def _prep_sparse(rows, cols, vals):
    rows = np.asarray(rows).astype(np.int64)
    cols = np.asarray(cols).astype(np.int64)
    vals = np.asarray(vals, np.float32)
    inb = rows < SEQ
    r, c, v = rows[inb], cols[inb], vals[inb]
    clamped = c >= SEQ
    rc, vc = r[clamped], v[clamped]
    cnt = np.bincount(rc, minlength=SEQ) if rc.size else np.zeros(SEQ, np.int64)
    C1 = max(int(cnt.max()) if rc.size else 0, 1)
    clampv = np.zeros((SEQ, C1), np.float32)
    off = np.zeros(SEQ, np.int64)
    for i in range(rc.shape[0]):
        rr = rc[i]
        clampv[rr, off[rr]] = vc[i]
        off[rr] += 1
    ri, ci, vi = r[~clamped], c[~clamped], v[~clamped]
    key = ri * SEQ + ci
    order = np.argsort(key, kind="stable")
    ri, ci, vi, key = ri[order], ci[order], vi[order], key[order]
    first = np.ones(key.shape[0], bool)
    if key.shape[0] > 1:
        first[1:] = key[1:] != key[:-1]
    S = np.zeros((SEQ, SEQ), np.float32)
    S[ri[first], ci[first]] = vi[first]
    rd, cd, vd = ri[~first], ci[~first], vi[~first]
    cnt2 = np.bincount(rd, minlength=SEQ) if rd.size else np.zeros(SEQ, np.int64)
    C2 = max(int(cnt2.max()) if rd.size else 0, 1)
    dupc = np.zeros((SEQ, C2), np.int64)
    dupv = np.zeros((SEQ, C2), np.float32)
    off2 = np.zeros(SEQ, np.int64)
    for i in range(rd.shape[0]):
        rr = rd[i]
        dupc[rr, off2[rr]] = cd[i]
        dupv[rr, off2[rr]] = vd[i]
        off2[rr] += 1
    cl = clampv.reshape(NST, 128, C1).transpose(1, 0, 2).reshape(128, NST * C1)
    return (np.ascontiguousarray(S), np.ascontiguousarray(cl), C1,
            dupc.reshape(-1), np.ascontiguousarray(dupv.reshape(1, -1)), C2)


def _conv_toeplitz(conv_w):
    w = np.asarray(conv_w, np.float32)[:, 0, :, 0]
    T = np.zeros((NHEAD, 3, 128, 128), np.float32)
    ii = np.arange(128)[:, None]
    jj = np.arange(128)[None, :]
    for d, dlt in ((0, -128), (1, 0), (2, 128)):
        t = (ii + dlt) - jj + 16
        m = (t >= 0) & (t < 33)
        tm = np.where(m, t, 0)
        for h in range(NHEAD):
            T[h, d] = np.where(m, w[h][tm], 0.0)
    return np.ascontiguousarray(T.transpose(2, 0, 1, 3).reshape(128, NHEAD * 3 * 128))


def _fm(a):
    return np.ascontiguousarray(np.asarray(a, np.float32).T)


def _fold_bias(b, nchunk):
    return np.ascontiguousarray(np.asarray(b, np.float32).reshape(nchunk, 128).T)


class Prog:

    def __init__(self):
        self.ops = {e: [] for e in ("sync", "gpsimd", "scalar", "vector", "tensor")}
        self.kinds = {e: [] for e in self.ops}
        self.attach = {e: {} for e in self.ops}
        self.cnt = {}

    def on(self, eng, fn, kind="op"):
        self.ops[eng].append(fn)
        self.kinds[eng].append(kind)

    def mark_attach(self, eng, sem, n):
        ks = self.kinds[eng]
        for j in range(len(ks) - 1, -1, -1):
            if ks[j] == "op":
                lst = self.attach[eng].setdefault(j, [])
                for i, (sem0, n0) in enumerate(lst):
                    if sem0 is sem:
                        lst[i] = (sem0, n0 + n)
                        return True
                if lst:
                    return False
                lst.append((sem, n))
                return True
            if ks[j] == "wait":
                continue
            return False
        return False

    def replay(self, eng, e):
        att = self.attach[eng]
        for i, fn in enumerate(self.ops[eng]):
            r = fn(e)
            for sem_, n in att.get(i, ()):
                try:
                    r.then_inc(sem_, n)
                except Exception:
                    e.sem_inc(sem_, n)

    def bump(self, sem, n):
        self.cnt[sem.name] = self.cnt.get(sem.name, 0) + n
        return self.cnt[sem.name]

    def val(self, sem):
        return self.cnt.get(sem.name, 0)


def build_nc(C1, C2, upto=None, dbg_ext=False):
    nc = bass.Bass("TRN2", target_bir_lowering=False, debug=False,
                   detect_race_conditions=False)
    dram = {}

    def din(name, shape, dt=F32):
        dram[name] = nc.dram_tensor(name, shape, dt, kind="ExternalInput").ap()
        return dram[name]

    def dint(name, shape, dt=F32):
        kind = "ExternalOutput" if dbg_ext else "Internal"
        dram[name] = nc.dram_tensor(name, shape, dt, kind=kind).ap()
        return dram[name]

    xb0 = din("xb0", [F, SEQ])
    xown = din("xown", [F, NT])
    mask_b0 = din("mask_b0", [F, SEQ], BF16)
    mask_own = din("mask_own", [F, NT], BF16)
    W1_d = din("W1", [HID, F])
    W2_d = din("W2", [F, HID])
    b1v = din("b1v", [128, HC])
    b2v = din("b2v", [128, KC])
    wvb_d = din("wvb", [128, KC])
    outb_d = din("outb", [128, KC])
    wq_d = din("wq", [F, 256])
    wk_d = din("wk", [F, 256])
    wv_d = din("wv", [F, F])
    qkv_d = din("qkvw", [F, 1536])
    outw_d = din("outw", [512, F])
    convT_d = din("convT", [128, NHEAD * 3 * 128])
    Sd = din("Sd", [SEQ, SEQ])
    clampv_d = din("clampv", [128, NST * C1])
    dupc_d = din("dupc", [128, (SEQ * C2) // 16], I16)
    dupv_d = din("dupv", [1, SEQ * C2])
    winmap_d = din("winmap", [128, WIN // 16], I16)
    ident_d = din("ident", [128, 128])
    onesrow_d = din("onesrow", [1, 128])
    onescol_d = din("onescol", [128, 1])
    yout = nc.dram_tensor("yout", [F, NT], F32, kind="ExternalOutput").ap()

    x1s = dint("x1s", [F, SEQ + NT])
    d0b_s = dint("d0b", [F, SEQ], BF16)
    d0f_s = dint("d0f", [F, SEQ])
    qkv_s = dint("qkvs", [1536, SEQ], BF16)
    vsm_s = dint("vsms", [SEQ, 512], BF16)
    xg0_s = dint("xg0s", [F, SEQ])
    asc_s = dint("asc", [1, SEQ])
    w2qs = dint("w2qs", [HID, F], FP8)
    dbg_xq = dint("dbg_xq", [128, KC, SLAB], FP8)
    dbg_hq = dint("dbg_hq", [128, HC, SLAB], FP8)
    dbg_yb = dint("dbg_yb", [128, KC, SLAB], BF16)
    dbg_w1 = dint("dbg_w1", [128, KC, HID], FP8)
    dbg_ms = dint("dbg_ms", [128, KC, SLAB], BF16)
    dbg_x1 = dint("dbg_x1", [128, KC, SLAB])
    dbg_qw = dint("dbg_qw", [128, KC, 1536], BF16)
    dbg_d0s = dint("dbg_d0s", [128, KC, SLAB], BF16)
    dbg_aw = dint("dbg_aw", [128, WIN])
    dbg_xw = dint("dbg_xw", [128, KC, WIN])
    dbg_dsl = dint("dbg_dsl", [128, KC, SLAB])
    dbg_val = dint("dbg_val", [128, KC, SLAB])
    dbg_x2 = dint("dbg_x2", [128, KC, SLAB])

    def sbt(name, shape, dt=F32):
        return nc.alloc_sbuf_tensor(name, shape, dt).ap()

    w1qT = sbt("w1qT", [128, KC, HID], FP8)
    ident = sbt("identt", [128, 128])
    identb = sbt("identb", [128, 128], BF16)
    onesrow = sbt("onesrowt", [1, 128])
    onescol = sbt("onescolt", [128, 1])
    onescol_b = sbt("onescol_b", [128, 1], BF16)
    b1q = sbt("b1q", [128, HC])
    b2q = sbt("b2q", [128, KC])
    wvb = sbt("wvbt", [128, KC])
    outb = sbt("outbt", [128, KC])
    convT = sbt("convTt", [128, NHEAD, 3, 128], BF16)
    winmap = sbt("winmapt", [128, WIN // 16], I16)
    alphaw = sbt("alphawt", [128, WIN])
    xg0w = sbt("xg0wt", [128, KC, WIN])
    small = sbt("small", [128, 16])
    srow = sbt("srow", [1, 4, SLAB])
    qlkl = sbt("qlkl", [128, 8, 256], BF16)
    t3b = sbt("t3b", [128, NHEAD, 2, 64], BF16)
    araw = sbt("araw", [128, NST])
    trf = sbt("trf", [128, 128])
    dupct = sbt("dupctt", [128, (SEQ * C2) // 16], I16)

    NAF, NAB, NAQ = 18432, 17408, 22528
    AFT = sbt("AF32", [128, NAF])
    ABT = sbt("ABF", [128, NAB], BF16)
    AQT = sbt("AF8", [128, NAQ], FP8)

    def f32s(off, a, b=None):
        if b is None:
            assert off + a <= NAF
            return AFT[:, off:off + a]
        assert off + a * b <= NAF
        return AFT[:, off:off + a * b].rearrange("p (a b) -> p a b", a=a)

    def bf16s(off, a, b=None):
        if b is None:
            assert off + a <= NAB
            return ABT[:, off:off + a]
        assert off + a * b <= NAB
        return ABT[:, off:off + a * b].rearrange("p (a b) -> p a b", a=a)

    def fp8s(off, a, b=None):
        if b is None:
            assert off + a <= NAQ
            return AQT[:, off:off + a]
        assert off + a * b <= NAQ
        return AQT[:, off:off + a * b].rearrange("p (a b) -> p a b", a=a)

    ps = [nc.alloc_psum_tensor(f"ps{i}", [128, SLAB], F32).ap() for i in range(8)]

    p = Prog()
    _sem_keep = []

    def mksem(name):
        cm = nc.semaphore(name)
        h = cm.__enter__()
        _sem_keep.append(cm)
        return h

    s_ld = mksem("s_ld")
    s_sp = mksem("s_sp")
    s_t = mksem("s_t")
    s_v = mksem("s_v")
    s_s = mksem("s_s")
    s_g = mksem("s_g")
    ENG = {"tensor": s_t, "vector": s_v, "scalar": s_s, "gpsimd": s_g}

    def ld(dst, src, slow=False):
        v = p.bump(s_ld, 16)
        p.on("sync", lambda e, dst=dst, src=src, slow=slow: e.dma_start(
            out=dst, in_=src, allow_slow_non_contiguous=slow).then_inc(s_ld, 16),
             kind="dma")
        return (s_ld, v)

    def stw(dst, src, slow=False):
        v = p.bump(s_sp, 16)
        p.on("gpsimd", lambda e, dst=dst, src=src, slow=slow: e.dma_start(
            out=dst, in_=src, allow_slow_non_contiguous=slow).then_inc(s_sp, 16),
             kind="dma")
        return (s_sp, v)

    def wait(eng, tok):
        if tok is None:
            return
        sem_, v = tok
        p.on(eng, lambda e, sem_=sem_, v=v: e.wait_ge(sem_, v), kind="wait")

    def mark(eng):
        sem_ = ENG[eng]
        v = p.bump(sem_, 1)
        if not p.mark_attach(eng, sem_, 1):
            p.on(eng, lambda e, sem_=sem_: e.sem_inc(sem_, 1), kind="wait")
        return (sem_, v)

    def fence(eng):
        wait(eng, mark(eng))

    ALL = ("sync", "gpsimd", "scalar", "vector", "tensor")

    def full_barrier():
        toks = [mark(e) for e in ("gpsimd", "scalar", "vector", "tensor")]
        lv, sv = p.val(s_ld), p.val(s_sp)
        for c in ALL:
            for t in toks:
                if t[0] is not ENG.get(c):
                    wait(c, t)
            p.on(c, lambda e, lv=lv: e.wait_ge(s_ld, lv), kind="wait")
            p.on(c, lambda e, sv=sv: e.wait_ge(s_sp, sv), kind="wait")

    _ckpts = []

    def ckpt():
        _ckpts.append(({e: len(p.ops[e]) for e in p.ops}, dict(p.cnt),
                       {e: {i: len(v) for i, v in p.attach[e].items()}
                        for e in p.ops}))

    _bank = [0]

    def bank():
        b = ps[_bank[0] % 4]
        _bank[0] += 1
        return b

    def V(fn):
        p.on("vector", fn)

    def S(fn):
        p.on("scalar", fn)

    def T(fn):
        p.on("tensor", fn)

    def G(fn):
        p.on("gpsimd", fn)

    def fmv(d):
        return d.rearrange("(a p) s -> p a s", p=128)

    def mmgroup(items):
        n = len(items)
        for i, (pb, lhsT, rhs) in enumerate(items):
            T(lambda e, pb=pb, lhsT=lhsT, rhs=rhs, st=(i == 0), sp=(i == n - 1):
              e.matmul(pb, lhsT, rhs, start=st, stop=sp))

    ld(ident, ident_d)
    ld(onesrow, onesrow_d)
    ld(onescol, onescol_d)
    ld(b1q, b1v)
    ld(b2q, b2v)
    ld(wvb, wvb_d)
    ld(outb, outb_d)
    ld(winmap, winmap_d)
    ld(dupct, dupc_d)
    cstage = f32s(0, NHEAD * 3 * 128)
    t_conv = ld(cstage, convT_d)
    wait("vector", t_conv)
    V(lambda e: e.tensor_copy(identb, ident))
    V(lambda e: e.tensor_copy(onescol_b, onescol))
    V(lambda e: e.tensor_copy(convT.rearrange("p h t q -> p (h t q)"), cstage))
    bq8 = fp8s(0, HC)
    V(lambda e: e.tensor_copy(bq8, b1q))
    fence("vector")
    V(lambda e: e.tensor_copy(b1q, bq8))
    bq8b = fp8s(HC, KC)
    V(lambda e: e.tensor_copy(bq8b, b2q))
    fence("vector")
    V(lambda e: e.tensor_copy(b2q, bq8b))
    full_barrier()

    wstage = f32s(0, 2, F)
    tok_use = None
    for ncx in range(HC):
        buf = wstage[:, ncx % 2, :]
        wait("sync", tok_use)
        t_l = ld(buf, W1_d[128 * ncx:128 * ncx + 128, :])
        wait("tensor", t_l)
        for kc in range(KC):
            pb = bank()
            T(lambda e, pb=pb, buf=buf, kc=kc: e.transpose(
                pb[:, 0:128], buf[:, 128 * kc:128 * kc + 128], ident))
            tmm = mark("tensor")
            wait("vector", tmm)
            dst = w1qT[:, kc, 128 * ncx:128 * ncx + 128]
            V(lambda e, dst=dst, pb=pb: e.tensor_copy(dst, pb[:, 0:128]))
            tv = mark("vector")
            wait("tensor", tv)
        tok_use = mark("vector")
    w2row = fp8s(0, F)
    for ncx in range(HC):
        buf = wstage[:, ncx % 2, :]
        wait("sync", tok_use)
        t_l = ld(buf.rearrange("p (a b) -> p a b", a=KC),
                 W2_d[:, 128 * ncx:128 * ncx + 128].rearrange("(a p) n -> p a n", p=128))
        wait("tensor", t_l)
        wait("vector", (s_sp, p.val(s_sp)))
        for oc in range(KC):
            pb = bank()
            T(lambda e, pb=pb, buf=buf, oc=oc: e.transpose(
                pb[:, 0:128], buf[:, 128 * oc:128 * oc + 128], ident))
            tmm = mark("tensor")
            wait("vector", tmm)
            V(lambda e, pb=pb, oc=oc: e.tensor_copy(w2row[:, 128 * oc:128 * oc + 128],
                                                    pb[:, 0:128]))
            tv = mark("vector")
            wait("tensor", tv)
        tv = mark("vector")
        wait("gpsimd", tv)
        stw(w2qs[128 * ncx:128 * ncx + 128, :], w2row)
        tok_use = mark("vector")
    full_barrier()
    ckpt()

    def emit_ln(xsl, xbsl, sqsl, tmp, out_f32=None, out_bf=None, out_f8=None,
                stats_only=False):
        for k in range(KC):
            V(lambda e, k=k: e.tensor_copy(xbsl[:, k, :], xsl[:, k, :]))
            S(lambda e, k=k: e.activation(sqsl[:, k, :], xsl[:, k, :], AF.Square))
        tv = mark("vector")
        tsx = mark("scalar")
        wait("tensor", tv)
        wait("tensor", tsx)
        for k in range(KC):
            T(lambda e, k=k: e.matmul(ps[4][0:1, :], onescol_b, xbsl[:, k, :],
                                      start=(k == 0), stop=(k == KC - 1)))
        for k in range(KC):
            T(lambda e, k=k: e.matmul(ps[5][0:1, :], onescol_b, sqsl[:, k, :],
                                      start=(k == 0), stop=(k == KC - 1)))
        tmm = mark("tensor")
        wait("vector", tmm)
        mr, rr, vv, zz = srow[:, 0, :], srow[:, 1, :], srow[:, 2, :], srow[:, 3, :]
        V(lambda e: e.tensor_scalar(mr, ps[4][0:1, :], 1.0 / F, None, Alu.mult))
        V(lambda e: e.tensor_scalar(vv, ps[5][0:1, :], 1.0 / F, None, Alu.mult))
        V(lambda e: e.tensor_tensor(zz, mr, mr, Alu.mult))
        V(lambda e: e.tensor_tensor(vv, vv, zz, Alu.subtract))
        V(lambda e: e.tensor_scalar(vv, vv, 1e-5, None, Alu.add))
        tv2 = mark("vector")
        wait("scalar", tv2)
        S(lambda e: e.activation(zz, vv, AF.Sqrt))
        ts2 = mark("scalar")
        wait("vector", ts2)
        V(lambda e: e.reciprocal(rr, zz))
        tv3 = mark("vector")
        wait("tensor", tv3)
        T(lambda e: e.matmul(ps[6], onesrow, mr, start=True, stop=True))
        T(lambda e: e.matmul(ps[7], onesrow, rr, start=True, stop=True))
        tb = mark("tensor")
        wait("vector", tb)
        if stats_only:
            return
        for k in range(KC):
            V(lambda e, k=k, tmp=tmp: e.tensor_tensor(tmp, xsl[:, k, :], ps[6], Alu.subtract))
            if out_f32 is not None:
                V(lambda e, k=k, tmp=tmp: e.tensor_tensor(out_f32[:, k, :], tmp, ps[7], Alu.mult))
                if out_bf is not None:
                    V(lambda e, k=k: e.tensor_copy(out_bf[:, k, :], out_f32[:, k, :]))
                if out_f8 is not None:
                    V(lambda e, k=k: e.tensor_copy(out_f8[:, k, :], out_f32[:, k, :]))
            elif out_bf is not None:
                V(lambda e, k=k, tmp=tmp: e.tensor_tensor(out_bf[:, k, :], tmp, ps[7], Alu.mult))
                if out_f8 is not None:
                    V(lambda e, k=k: e.tensor_copy(out_f8[:, k, :], out_bf[:, k, :]))
            else:
                V(lambda e, k=k, tmp=tmp: e.tensor_tensor(out_f8[:, k, :], tmp, ps[7], Alu.mult))

    def emit_ffn(xq, hbuf, hq, ybsl):
        toks = []
        for ncx in range(HC):
            if ncx >= 3:
                wait("tensor", toks[ncx - 3])
            pb = bank()
            for k in range(KC):
                T(lambda e, pb=pb, ncx=ncx, k=k: e.matmul(
                    pb, w1qT[:, k, 128 * ncx:128 * ncx + 128], xq[:, k, :],
                    start=(k == 0), stop=(k == KC - 1)))
            tmm = mark("tensor")
            wait("scalar", tmm)
            bias = b1q[:, ncx:ncx + 1]
            S(lambda e, pb=pb, bias=bias, hbuf=hbuf: e.activation(
                hbuf, pb, AF.Identity, bias=bias))
            S(lambda e, ncx=ncx, hbuf=hbuf, hq=hq: e.activation(
                hq[:, ncx, :], hbuf, AF.Relu))
            toks.append(mark("scalar"))
        wait("tensor", toks[-1])
        donetoks = []
        for ncx in range(HC):
            wbuf = w2sb[:, ncx % 2, :]
            if ncx >= 2:
                wait("sync", donetoks[ncx - 2])
            tok_l = ld(wbuf, w2qs[128 * ncx:128 * ncx + 128, :])
            wait("tensor", tok_l)
            for ocx in range(KC):
                T(lambda e, ocx=ocx, ncx=ncx, wbuf=wbuf: e.matmul(
                    ps[ocx], wbuf[:, 128 * ocx:128 * ocx + 128], hq[:, ncx, :],
                    start=(ncx == 0), stop=(ncx == HC - 1)))
            donetoks.append(mark("tensor"))
        tmm = mark("tensor")
        wait("scalar", tmm)
        for ocx in range(KC):
            bias = b2q[:, ocx:ocx + 1]
            S(lambda e, ocx=ocx, bias=bias, ybsl=ybsl: e.activation(
                ybsl[:, ocx, :], ps[ocx], AF.Identity, bias=bias))
        ts_f = mark("scalar")
        wait("tensor", ts_f)

    def soft_rows(bufs, outs):
        nb = len(bufs)
        for i, b in enumerate(bufs):
            V(lambda e, b=b, i=i: e.tensor_reduce(small[:, i:i + 1], b, AX.X,
                                                  Alu.max, negate=True))
        tv = mark("vector")
        wait("scalar", tv)
        for i, b in enumerate(bufs):
            S(lambda e, b=b, i=i: e.activation(b, b, AF.Exp, bias=small[:, i:i + 1]))
        ts = mark("scalar")
        wait("vector", ts)
        for i, b in enumerate(bufs):
            V(lambda e, b=b, i=i: e.tensor_reduce(small[:, 8 + i:9 + i], b, AX.X, Alu.add))
        fence("vector")
        V(lambda e, nb=nb: e.reciprocal(small[:, 8:8 + nb], small[:, 8:8 + nb]))
        fence("vector")
        for i, (b, o) in enumerate(zip(bufs, outs)):
            V(lambda e, b=b, o=o, i=i: e.tensor_scalar(o, b, small[:, 8 + i:9 + i],
                                                       None, Alu.mult))

    def tr128(dst_slice, src_bf_tile):
        V(lambda e, src_bf_tile=src_bf_tile: e.tensor_copy(trf, src_bf_tile))
        tv = mark("vector")
        wait("tensor", tv)
        pb = bank()
        T(lambda e, pb=pb: e.transpose(pb[:, 0:128], trf, ident))
        tmm = mark("tensor")
        wait("vector", tmm)
        V(lambda e, pb=pb, dst_slice=dst_slice: e.tensor_copy(dst_slice, pb[:, 0:128]))
        tv = mark("vector")
        wait("tensor", tv)

    def transpose256(dstb, srcb):
        for ic in range(2):
            for jc in range(2):
                tr128(dstb[:, jc, 128 * ic:128 * ic + 128],
                      srcb[:, ic, 128 * jc:128 * jc + 128])

    def mm256(dst_f32, lhsTb, rhsb):
        for ic in range(2):
            pb = ps[ic]
            for kc2 in range(2):
                T(lambda e, pb=pb, ic=ic, kc2=kc2, lhsTb=lhsTb, rhsb=rhsb: e.matmul(
                    pb[:, 0:256], lhsTb[:, kc2, 128 * ic:128 * ic + 128],
                    rhsb[:, kc2, :], start=(kc2 == 0), stop=(kc2 == 1)))
        tmm = mark("tensor")
        wait("vector", tmm)
        for ic in range(2):
            V(lambda e, ic=ic, dst_f32=dst_f32: e.tensor_copy(dst_f32[:, ic, :],
                                                              ps[ic][:, 0:256]))
        tv = mark("vector")
        wait("tensor", tv)

    xslab = f32s(0, KC, SLAB)
    x1slab = f32s(4096, KC, SLAB)
    tmp1 = f32s(8192, SLAB)
    d0fs = f32s(8704, KC, SLAB)
    xbslab = bf16s(0, KC, SLAB)
    sqslab = bf16s(4096, KC, SLAB)
    mslab = bf16s(8192, KC, SLAB)
    ybslab = bf16s(12288, KC, SLAB)
    hbuf = bf16s(16384, SLAB)
    d0bs = bf16s(8192, KC, SLAB)
    xq = fp8s(0, KC, SLAB)
    hq = fp8s(4096, HC, SLAB)
    w2sb = fp8s(20480, 2, F)

    for sl in range(16):
        isb0 = sl < 8
        src = xb0 if isb0 else xown
        msrc = mask_b0 if isb0 else mask_own
        c0 = SLAB * (sl if isb0 else sl - 8)
        t_x = ld(xslab, fmv(src)[:, :, c0:c0 + SLAB])
        t_m = ld(mslab, fmv(msrc)[:, :, c0:c0 + SLAB])
        wait("vector", t_x)
        wait("scalar", t_x)
        emit_ln(xslab, xbslab, sqslab, tmp1, out_f8=xq)
        tv = mark("vector")
        wait("tensor", tv)
        if sl == 0:
            wait("gpsimd", tv)
            stw(dram["dbg_xq"], xq)
            stw(dram["dbg_w1"], w1qT)
        emit_ffn(xq, hbuf, hq, ybslab)
        ts_dbg = mark("scalar")
        if sl == 0:
            wait("gpsimd", ts_dbg)
            stw(dram["dbg_hq"], hq)
            stw(dram["dbg_yb"], ybslab)
        wait("vector", ts_dbg)
        wait("vector", t_m)
        for k in range(KC):
            V(lambda e, k=k: e.tensor_tensor(tmp1, ybslab[:, k, :], mslab[:, k, :], Alu.mult))
            V(lambda e, k=k: e.tensor_tensor(x1slab[:, k, :], xslab[:, k, :], tmp1, Alu.add))
        tv = mark("vector")
        wait("gpsimd", tv)
        if sl == 0:
            stw(dram["dbg_ms"], mslab)
            stw(dram["dbg_x1"], x1slab)
        stw(fmv(x1s)[:, :, sl * SLAB:sl * SLAB + SLAB], x1slab)
        if isb0:
            wait("scalar", tv)
            emit_ln(x1slab, xbslab, sqslab, tmp1, out_f32=d0fs, out_bf=d0bs)
            tv2 = mark("vector")
            wait("gpsimd", tv2)
            stw(fmv(d0f_s)[:, :, c0:c0 + SLAB], d0fs)
            stw(fmv(d0b_s)[:, :, c0:c0 + SLAB], d0bs)
        full_barrier()
    ckpt()

    qkvw_bf = bf16s(0, KC, 1536)
    wstage2 = f32s(0, KC, 1536)
    t_w = ld(wstage2, fmv(qkv_d))
    wait("vector", t_w)
    for k in range(KC):
        V(lambda e, k=k: e.tensor_copy(qkvw_bf[:, k, :], wstage2[:, k, :]))
    tvq = mark("vector")
    wait("gpsimd", tvq)
    stw(dram["dbg_qw"], qkvw_bf)
    full_barrier()

    d0bstage = bf16s(12288, KC, SLAB)
    qtile = bf16s(16384, SLAB)
    vtile = bf16s(16896, 128)
    evacf = f32s(0, SLAB)
    lmacc = f32s(512, 8, 256)
    st_q = None
    st_v = None
    prev_t = None
    for stile in range(8):
        c0 = SLAB * stile
        wait("sync", prev_t)
        sv_now = p.val(s_sp)
        p.on("sync", lambda e, sv_now=sv_now: e.wait_ge(s_sp, sv_now),
             kind="wait")
        t_d = ld(d0bstage, fmv(d0b_s)[:, :, c0:c0 + SLAB])
        wait("tensor", t_d)
        if stile == 0:
            wait("gpsimd", t_d)
            stw(dram["dbg_d0s"], d0bstage)
        for qf in range(12):
            pb = bank()
            for k in range(KC):
                T(lambda e, pb=pb, qf=qf, k=k: e.matmul(
                    pb, qkvw_bf[:, k, 128 * qf:128 * qf + 128], d0bstage[:, k, :],
                    start=(k == 0), stop=(k == KC - 1)))
            tmm = mark("tensor")
            wait("scalar", tmm)
            scale = 0.125 if qf < 4 else 1.0
            S(lambda e, pb=pb, scale=scale: e.activation(evacf, pb, AF.Identity, scale=scale))
            ts = mark("scalar")
            wait("vector", ts)
            wait("vector", st_q)
            V(lambda e: e.tensor_copy(qtile, evacf))
            if qf < 8:
                lm = lmacc[:, qf, 32 * stile:32 * stile + 32]
                V(lambda e, lm=lm: e.tensor_reduce(
                    lm, evacf.rearrange("p (a b) -> p a b", b=16), AX.X, Alu.add))
            tv = mark("vector")
            wait("gpsimd", tv)
            st_q = stw(qkv_s[128 * qf:128 * qf + 128, c0:c0 + SLAB], qtile)
            if qf >= 8:
                wait("tensor", ts)
                for tt in range(4):
                    pb2 = bank()
                    T(lambda e, pb2=pb2, tt=tt: e.transpose(
                        pb2[:, 0:128], evacf[:, 128 * tt:128 * tt + 128], ident))
                    tmm2 = mark("tensor")
                    wait("vector", tmm2)
                    wait("vector", st_v)
                    V(lambda e, pb2=pb2: e.tensor_copy(vtile, pb2[:, 0:128]))
                    tv2 = mark("vector")
                    wait("gpsimd", tv2)
                    st_v = stw(vsm_s[c0 + 128 * tt:c0 + 128 * tt + 128,
                                     128 * (qf - 8):128 * (qf - 8) + 128], vtile)
                    wait("tensor", tv2)
            wait("tensor", tv)
        prev_t = mark("tensor")
        tv_end = mark("vector")
        wait("scalar", prev_t)
        wait("scalar", tv_end)
    fence("vector")
    for qf in range(8):
        V(lambda e, qf=qf: e.tensor_scalar(qlkl[:, qf, :], lmacc[:, qf, :],
                                           1.0 / 16.0, None, Alu.mult))
    full_barrier()
    ckpt()

    kh_b = bf16s(0, SEQ)
    a3b = bf16s(4096, 2, SEQ)
    vhsm = bf16s(12288, NST, 64)
    zb = bf16s(14336, 2, 256)
    zcur_b = bf16s(14848, 2, 256)
    XT = bf16s(15360, 2, 256)
    wrkb = bf16s(15872, 2, 256)
    a3Tt = bf16s(16384, 2, 128)
    t2bb = bf16s(16640, 2, 64)
    scrT = bf16s(16768, 2, 256)
    att3f = f32s(0, 2, SEQ)
    zmat = f32s(8192, 2, 256)
    xzf = f32s(8704, 2, 256)
    wrkf = f32s(9216, 2, 256)

    for h in range(NHEAD):
        hcol = 64 * (h % 2)
        hch = h // 2
        qlh = qlkl[hcol:hcol + 64, hch, :]
        klh = qlkl[hcol:hcol + 64, 4 + hch, :]
        for lc in range(2):
            pb = ps[lc]
            T(lambda e, pb=pb, lc=lc, qlh=qlh, klh=klh: e.matmul(
                pb[:, 0:256], qlh[:, 128 * lc:128 * lc + 128], klh,
                start=True, stop=True))
        tmm = mark("tensor")
        wait("vector", tmm)
        for lc in range(2):
            V(lambda e, lc=lc: e.tensor_copy(zmat[:, lc, :], ps[lc][:, 0:256]))
        tv = mark("vector")
        wait("scalar", tv)
        wait("tensor", tv)
        soft_rows([zmat[:, 0, :], zmat[:, 1, :]], [zb[:, 0, :], zb[:, 1, :]])
        tv = mark("vector")
        wait("tensor", tv)
        wait("scalar", tv)
        for lc in range(2):
            V(lambda e, lc=lc: e.tensor_reduce(small[:, lc:lc + 1], zb[:, lc, :],
                                               AX.X, Alu.add))
        tv = mark("vector")
        wait("tensor", tv)
        for lc in range(2):
            pb = bank()
            T(lambda e, pb=pb, lc=lc: e.transpose(pb[0:1, 0:128], small[:, lc:lc + 1], ident))
            tmm = mark("tensor")
            wait("vector", tmm)
            V(lambda e, pb=pb, lc=lc: e.tensor_copy(srow[:, 0, 128 * lc:128 * lc + 128],
                                                    pb[0:1, 0:128]))
            tv = mark("vector")
            wait("tensor", tv)
        pb = ps[2]
        for lc in range(2):
            T(lambda e, pb=pb, lc=lc: e.matmul(pb[0:1, 0:256], onescol_b, zb[:, lc, :],
                                               start=(lc == 0), stop=(lc == 1)))
        tmm = mark("tensor")
        wait("vector", tmm)
        V(lambda e, pb=pb: e.tensor_copy(srow[:, 1, 0:256], pb[0:1, 0:256]))
        fence("vector")
        V(lambda e: e.tensor_reduce(srow[:, 2, 0:1], srow[:, 0, 0:256], AX.X, Alu.max))
        V(lambda e: e.tensor_reduce(srow[:, 2, 1:2], srow[:, 1, 0:256], AX.X, Alu.max))
        fence("vector")
        V(lambda e: e.tensor_tensor(srow[:, 2, 2:3], srow[:, 2, 0:1], srow[:, 2, 1:2],
                                    Alu.mult))
        fence("vector")
        V(lambda e: e.reciprocal(srow[:, 2, 3:4], srow[:, 2, 2:3]))
        tv = mark("vector")
        wait("tensor", tv)
        T(lambda e: e.matmul(ps[3][:, 0:1], onesrow, srow[:, 2, 3:4], start=True, stop=True))
        tmm = mark("tensor")
        wait("vector", tmm)
        V(lambda e: e.tensor_copy(small[:, 12:13], ps[3][:, 0:1]))
        tv = mark("vector")
        wait("tensor", tv)
        for ic in range(2):
            for jc in range(2):
                V(lambda e, ic=ic, jc=jc: e.tensor_copy(
                    trf, zb[:, ic, 128 * jc:128 * jc + 128]))
                tv = mark("vector")
                wait("tensor", tv)
                pb = bank()
                T(lambda e, pb=pb: e.transpose(pb[:, 0:128], trf, ident))
                tmm = mark("tensor")
                wait("scalar", tmm)
                S(lambda e, pb=pb, ic=ic, jc=jc: e.activation(
                    wrkf[:, jc, 128 * ic:128 * ic + 128], pb[:, 0:128], AF.Identity,
                    scale=small[:, 12:13]))
                ts = mark("scalar")
                wait("tensor", ts)
                wait("vector", ts)
        ts = mark("scalar")
        wait("vector", ts)
        for lc in range(2):
            V(lambda e, lc=lc: e.tensor_copy(zcur_b[:, lc, :], wrkf[:, lc, :]))
        tv = mark("vector")
        wait("tensor", tv)
        transpose256(XT, zb)
        for it in range(6):
            mm256(xzf, XT, zcur_b)
            wait("scalar", mark("vector"))
            for lc in range(2):
                S(lambda e, lc=lc: e.activation(wrkf[:, lc, :], xzf[:, lc, :],
                                                AF.Identity, scale=-1.0))
            ts = mark("scalar")
            wait("vector", ts)
            for lc in range(2):
                V(lambda e, lc=lc: e.scalar_tensor_tensor(
                    wrkf[:, lc, 128 * lc:128 * lc + 128], ident, 7.0,
                    wrkf[:, lc, 128 * lc:128 * lc + 128], Alu.mult, Alu.add))
            fence("vector")
            for lc in range(2):
                V(lambda e, lc=lc: e.tensor_copy(wrkb[:, lc, :], wrkf[:, lc, :]))
                V(lambda e, lc=lc: e.tensor_copy(zb[:, lc, :], xzf[:, lc, :]))
            tv = mark("vector")
            wait("tensor", tv)
            transpose256(scrT, zb)
            mm256(xzf, scrT, wrkb)
            wait("scalar", mark("vector"))
            for lc in range(2):
                S(lambda e, lc=lc: e.activation(wrkf[:, lc, :], xzf[:, lc, :],
                                                AF.Identity, scale=-1.0))
            ts = mark("scalar")
            wait("vector", ts)
            for lc in range(2):
                V(lambda e, lc=lc: e.scalar_tensor_tensor(
                    wrkf[:, lc, 128 * lc:128 * lc + 128], ident, 15.0,
                    wrkf[:, lc, 128 * lc:128 * lc + 128], Alu.mult, Alu.add))
            fence("vector")
            for lc in range(2):
                V(lambda e, lc=lc: e.tensor_copy(wrkb[:, lc, :], wrkf[:, lc, :]))
            tv = mark("vector")
            wait("tensor", tv)
            mm256(xzf, scrT, wrkb)
            wait("scalar", mark("vector"))
            for lc in range(2):
                S(lambda e, lc=lc: e.activation(wrkf[:, lc, :], xzf[:, lc, :],
                                                AF.Identity, scale=-1.0))
            ts = mark("scalar")
            wait("vector", ts)
            for lc in range(2):
                V(lambda e, lc=lc: e.scalar_tensor_tensor(
                    wrkf[:, lc, 128 * lc:128 * lc + 128], ident, 13.0,
                    wrkf[:, lc, 128 * lc:128 * lc + 128], Alu.mult, Alu.add))
            fence("vector")
            for lc in range(2):
                V(lambda e, lc=lc: e.tensor_copy(wrkb[:, lc, :], wrkf[:, lc, :]))
            tv = mark("vector")
            wait("tensor", tv)
            transpose256(scrT, zcur_b)
            mm256(xzf, scrT, wrkb)
            wait("scalar", mark("vector"))
            for lc in range(2):
                S(lambda e, lc=lc: e.activation(zmat[:, lc, :], xzf[:, lc, :],
                                                AF.Identity, scale=0.25))
            ts = mark("scalar")
            wait("vector", ts)
            for lc in range(2):
                V(lambda e, lc=lc: e.tensor_copy(zcur_b[:, lc, :], zmat[:, lc, :]))
            tv = mark("vector")
            wait("tensor", tv)
        a2iT = scrT
        transpose256(a2iT, zcur_b)
        t_k = ld(kh_b[hcol:hcol + 64, :], qkv_s[512 + 64 * h:576 + 64 * h, :])
        wait("tensor", t_k)
        for lc in range(2):
            for stl in range(8):
                pb = bank()
                T(lambda e, pb=pb, lc=lc, stl=stl, qlh=qlh, hcol=hcol: e.matmul(
                    pb, qlh[:, 128 * lc:128 * lc + 128],
                    kh_b[hcol:hcol + 64, SLAB * stl:SLAB * stl + SLAB],
                    start=True, stop=True))
                tmm = mark("tensor")
                wait("vector", tmm)
                V(lambda e, pb=pb, lc=lc, stl=stl: e.tensor_copy(
                    att3f[:, lc, SLAB * stl:SLAB * stl + SLAB], pb))
                tv = mark("vector")
                wait("tensor", tv)
        soft_rows([att3f[:, 0, :], att3f[:, 1, :]], [a3b[:, 0, :], a3b[:, 1, :]])
        tv = mark("vector")
        wait("tensor", tv)
        wait("sync", tv)
        t_v = ld(vhsm, vsm_s[:, 64 * h:64 * h + 64].rearrange("(a p) f -> p a f", p=128))
        wait("tensor", t_v)
        for sc in range(NST):
            for lc in range(2):
                tr128(a3Tt[:, lc, :], a3b[:, lc, 128 * sc:128 * sc + 128])
            for lc in range(2):
                T(lambda e, sc=sc, lc=lc: e.matmul(
                    ps[4 + lc][:, 0:64], a3Tt[:, lc, :], vhsm[:, sc, :],
                    start=(sc == 0), stop=(sc == NST - 1)))
        tmm = mark("tensor")
        wait("vector", tmm)
        for lc in range(2):
            V(lambda e, lc=lc: e.tensor_copy(t2bb[:, lc, :], ps[4 + lc][:, 0:64]))
        tv = mark("vector")
        wait("tensor", tv)
        for lc in range(2):
            pb = bank()
            for kc2 in range(2):
                T(lambda e, pb=pb, lc=lc, kc2=kc2: e.matmul(
                    pb[:, 0:64], a2iT[:, kc2, 128 * lc:128 * lc + 128], t2bb[:, kc2, :],
                    start=(kc2 == 0), stop=(kc2 == 1)))
            tmm = mark("tensor")
            wait("vector", tmm)
            V(lambda e, pb=pb, h=h, lc=lc: e.tensor_copy(t3b[:, h, lc, :], pb[:, 0:64]))
            tv = mark("vector")
            wait("tensor", tv)
        full_barrier()
    ckpt()

    qgrp = bf16s(0, 4, SLAB)
    vgrp = bf16s(2048, 6, SLAB)
    a1all = bf16s(5120, 8, 256)
    a1T = bf16s(7168, 2, 128)
    outwb = bf16s(7424, 4, F)
    outrB = bf16s(11520, SLAB)
    outrT = bf16s(12032, 4, 128)
    a1f = f32s(0, 8, 256)
    xg0cols = f32s(2048, KC, 128)
    outrF = f32s(3072, SLAB)
    wstage3 = f32s(4096, 4, F)
    t_w = ld(wstage3, outw_d.rearrange("(a p) f -> p a f", p=128))
    wait("vector", t_w)
    for a in range(4):
        V(lambda e, a=a: e.tensor_copy(outwb[:, a, :], wstage3[:, a, :]))
    full_barrier()

    for sg in range(8):
        c0 = SLAB * sg
        lo = c0 - 128
        t_q = ld(qgrp, qkv_s[0:512, c0:c0 + SLAB].rearrange("(a p) s -> p a s", p=128))
        V(lambda e: e.memset(vgrp.rearrange("p a b -> p (a b)"), 0.0))
        tv0 = mark("vector")
        wait("sync", tv0)
        for ch in range(6):
            r0 = lo + 128 * ch
            if r0 < 0 or r0 >= SEQ:
                continue
            ld(vgrp[:, ch, :], vsm_s[r0:r0 + 128, :])
        t_vl = (s_ld, p.val(s_ld))
        wait("tensor", t_q)
        wait("tensor", t_vl)
        for sc4 in range(4):
            scg = 4 * sg + sc4
            pbout = ps[6]
            for h in range(NHEAD):
                hcol = 64 * (h % 2)
                hch = h // 2
                pb = bank()
                T(lambda e, pb=pb, h=h, hcol=hcol, hch=hch, sc4=sc4: e.matmul(
                    pb[:, 0:256], qgrp[hcol:hcol + 64, hch, 128 * sc4:128 * sc4 + 128],
                    qlkl[hcol:hcol + 64, 4 + hch, :], start=True, stop=True))
                tmm = mark("tensor")
                wait("vector", tmm)
                V(lambda e, pb=pb, h=h: e.tensor_copy(a1f[:, h, :], pb[:, 0:256]))
                tv = mark("vector")
                wait("tensor", tv)
            soft_rows([a1f[:, h, :] for h in range(NHEAD)],
                      [a1all[:, h, :] for h in range(NHEAD)])
            tv = mark("vector")
            wait("tensor", tv)
            group = []
            for h in range(NHEAD):
                for lc in range(2):
                    group.append(("a1", h, lc))
                for dd in range(3):
                    ch = sc4 + dd
                    r0 = lo + 128 * ch
                    if r0 < 0 or r0 >= SEQ:
                        continue
                    group.append(("cv", h, dd, ch))
            ng = len(group)
            gi = [0]
            for h in range(NHEAD):
                for lc in range(2):
                    tr128(a1T[:, lc, :], a1all[:, h, 128 * lc:128 * lc + 128])
                    i = gi[0]
                    gi[0] += 1
                    T(lambda e, h=h, lc=lc, pbout=pbout, st=(i == 0), sp=(i == ng - 1):
                      e.matmul(pbout[:, 64 * h:64 * h + 64], a1T[:, lc, :],
                               t3b[:, h, lc, :], start=st, stop=sp))
                for dd in range(3):
                    ch = sc4 + dd
                    r0 = lo + 128 * ch
                    if r0 < 0 or r0 >= SEQ:
                        continue
                    i = gi[0]
                    gi[0] += 1
                    T(lambda e, h=h, dd=dd, ch=ch, pbout=pbout, st=(i == 0), sp=(i == ng - 1):
                      e.matmul(pbout[:, 64 * h:64 * h + 64], convT[:, h, dd, :],
                               vgrp[:, ch, 64 * h:64 * h + 64], start=st, stop=sp))
            tmm = mark("tensor")
            wait("vector", tmm)
            V(lambda e, pbout=pbout: e.tensor_copy(outrF, pbout))
            tv = mark("vector")
            wait("tensor", tv)
            for fd in range(4):
                pb = bank()
                T(lambda e, pb=pb, fd=fd: e.transpose(
                    pb[:, 0:128], outrF[:, 128 * fd:128 * fd + 128], ident))
                tmm = mark("tensor")
                wait("vector", tmm)
                V(lambda e, pb=pb, fd=fd: e.tensor_copy(outrT[:, fd, :], pb[:, 0:128]))
                tv = mark("vector")
                wait("tensor", tv)
            for f2 in range(KC):
                pb = bank()
                for fd in range(4):
                    T(lambda e, pb=pb, f2=f2, fd=fd: e.matmul(
                        pb[:, 0:128], outwb[:, fd, 128 * f2:128 * f2 + 128],
                        outrT[:, fd, :], start=(fd == 0), stop=(fd == 3)))
                tmm = mark("tensor")
                wait("scalar", tmm)
                S(lambda e, pb=pb, f2=f2: e.activation(
                    xg0cols[:, f2, :], pb[:, 0:128], AF.Identity,
                    bias=outb[:, f2:f2 + 1]))
                ts = mark("scalar")
                wait("tensor", ts)
            ts = mark("scalar")
            wait("gpsimd", ts)
            stw(fmv(xg0_s)[:, :, 128 * scg:128 * scg + 128], xg0cols)
            wait("scalar", (s_sp, p.val(s_sp)))
        full_barrier()
    ckpt()

    q0f = f32s(0, 2, SEQ)
    k0f = f32s(8192, 2, SEQ)
    wstage4 = f32s(16384, KC, 256)
    wqb = bf16s(8192, KC, 256)
    wkb = bf16s(10240, KC, 256)
    enc0b = bf16s(0, KC, 1024)
    t_w = ld(wstage4, fmv(wq_d))
    wait("vector", t_w)
    for k in range(KC):
        V(lambda e, k=k: e.tensor_copy(wqb[:, k, :], wstage4[:, k, :]))
    tvw = mark("vector")
    wait("sync", tvw)
    t_w = ld(wstage4, fmv(wk_d))
    wait("vector", t_w)
    for k in range(KC):
        V(lambda e, k=k: e.tensor_copy(wkb[:, k, :], wstage4[:, k, :]))
    full_barrier()

    xga = f32s(16384, 1024)
    dfa = f32s(17408, 1024)
    for qt in range(4):
        s0 = 1024 * qt
        for k in range(KC):
            t_a = ld(xga, fmv(xg0_s)[:, k, s0:s0 + 1024])
            t_b = ld(dfa, fmv(d0f_s)[:, k, s0:s0 + 1024])
            wait("vector", t_a)
            wait("vector", t_b)
            V(lambda e, k=k: e.tensor_tensor(enc0b[:, k, :], xga, dfa, Alu.add))
            tv = mark("vector")
            wait("sync", tv)
        tv = mark("vector")
        wait("tensor", tv)
        for qf in range(2):
            for stl in range(2):
                pb = bank()
                for k in range(KC):
                    T(lambda e, pb=pb, qf=qf, stl=stl, k=k: e.matmul(
                        pb, wqb[:, k, 128 * qf:128 * qf + 128],
                        enc0b[:, k, SLAB * stl:SLAB * stl + SLAB],
                        start=(k == 0), stop=(k == KC - 1)))
                tmm = mark("tensor")
                wait("scalar", tmm)
                S(lambda e, pb=pb, qf=qf, stl=stl, s0=s0: e.activation(
                    q0f[:, qf, s0 + SLAB * stl:s0 + SLAB * stl + SLAB], pb,
                    AF.Identity, scale=0.0625))
                ts = mark("scalar")
                wait("tensor", ts)
                pb = bank()
                for k in range(KC):
                    T(lambda e, pb=pb, qf=qf, stl=stl, k=k: e.matmul(
                        pb, wkb[:, k, 128 * qf:128 * qf + 128],
                        enc0b[:, k, SLAB * stl:SLAB * stl + SLAB],
                        start=(k == 0), stop=(k == KC - 1)))
                tmm = mark("tensor")
                wait("scalar", tmm)
                S(lambda e, pb=pb, qf=qf, stl=stl, s0=s0: e.activation(
                    k0f[:, qf, s0 + SLAB * stl:s0 + SLAB * stl + SLAB], pb, AF.Identity))
                ts = mark("scalar")
                wait("tensor", ts)
        wait("vector", mark("scalar"))
        tv = mark("vector")
        wait("sync", tv)
    q0b = bf16s(0, 2, SEQ)
    k0b = bf16s(8192, 2, SEQ)
    wait("vector", mark("scalar"))
    for qf in range(2):
        V(lambda e, qf=qf: e.tensor_copy(q0b[:, qf, :], q0f[:, qf, :]))
        V(lambda e, qf=qf: e.tensor_copy(k0b[:, qf, :], k0f[:, qf, :]))
    full_barrier()
    ckpt()

    Dst = f32s(0, SEQ)
    DS = f32s(4096, SEQ)
    Sst = f32s(8192, SEQ)
    clampvt = f32s(12288, NST * C1)
    t_cl = ld(clampvt, clampv_d)
    wait("vector", t_cl)
    for st in range(NST):
        t_s = ld(Sst, Sd[128 * st:128 * st + 128, :])
        for ct in range(8):
            pb = bank()
            for qf in range(2):
                T(lambda e, pb=pb, st=st, ct=ct, qf=qf: e.matmul(
                    pb, q0b[:, qf, 128 * st:128 * st + 128],
                    k0b[:, qf, SLAB * ct:SLAB * ct + SLAB],
                    start=(qf == 0), stop=(qf == 1)))
            tmm = mark("tensor")
            wait("vector", tmm)
            V(lambda e, pb=pb, ct=ct: e.tensor_copy(Dst[:, SLAB * ct:SLAB * ct + SLAB], pb))
            tv = mark("vector")
            wait("tensor", tv)
        wait("vector", t_s)
        V(lambda e: e.tensor_tensor(DS, Dst, Sst, Alu.mult))
        V(lambda e, st=st: e.tensor_reduce(araw[:, st:st + 1], DS, AX.X, Alu.add))
        V(lambda e, st=st: e.tensor_reduce(
            small[:, 0:1], clampvt.rearrange("p (a b) -> p a b", a=NST)[:, st, :],
            AX.X, Alu.add))
        fence("vector")
        V(lambda e: e.tensor_tensor(small[:, 1:2], small[:, 0:1],
                                    Dst[:, SEQ - 1:SEQ], Alu.mult))
        fence("vector")
        V(lambda e, st=st: e.tensor_tensor(araw[:, st:st + 1], araw[:, st:st + 1],
                                           small[:, 1:2], Alu.add))
        tv = mark("vector")
        wait("sync", tv)
        wait("tensor", tv)
    full_barrier()

    EDG = SEQ * C2
    dupvt = f32s(0, EDG)
    prodf = f32s(EDG, EDG)
    arowf = f32s(2 * EDG, SEQ)
    arept = f32s(2 * EDG + SEQ, NST)
    t_dv = ld(dupvt[0:1, :], dupv_d)
    k0g = f32s(8192, 2, SEQ)
    for qf in range(2):
        V(lambda e, qf=qf: e.tensor_copy(k0g[:, qf, :], k0b[:, qf, :]))
    kgt = f32s(16384, 2, SLAB)
    mult = f32s(17408, SLAB)
    tvk = mark("vector")
    wait("gpsimd", tvk)
    p.on("gpsimd", lambda e: e.load_library(_APG_LIB), kind="dma")
    ntile = EDG // SLAB
    for tl in range(ntile):
        a0 = SLAB * tl
        idxsl = dupct[:, (a0 // 16):(a0 // 16) + SLAB // 16]
        for qf in range(2):
            G(lambda e, qf=qf, idxsl=idxsl: e.ap_gather(
                kgt[:, qf, :], k0g[:, qf, :], idxsl,
                channels=128, num_elems=SEQ, d=1, num_idxs=SLAB))
        tg = mark("gpsimd")
        wait("vector", tg)
        q0bc = [q0b[:, qf, :].rearrange("p (s o) -> p s o", o=1)
                .broadcast_to([128, SEQ, C2]).rearrange("p s c -> p (s c)")[:, a0:a0 + SLAB]
                for qf in range(2)]
        pb = ps[4]
        for qf in range(2):
            V(lambda e, qf=qf, q0bc=q0bc: e.tensor_tensor(mult, kgt[:, qf, :],
                                                          q0bc[qf], Alu.mult))
            tv = mark("vector")
            wait("tensor", tv)
            T(lambda e, pb=pb, qf=qf: e.matmul(pb[0:1, :], onescol, mult,
                                               start=(qf == 0), stop=(qf == 1)))
            tmm = mark("tensor")
            if qf == 0:
                wait("vector", tmm)
        tmm = mark("tensor")
        wait("vector", tmm)
        V(lambda e, pb=pb, a0=a0: e.tensor_copy(prodf[0:1, a0:a0 + SLAB], pb[0:1, :]))
        tv = mark("vector")
        wait("gpsimd", tv)
    p.on("gpsimd", lambda e: e.load_library(_STD_LIB), kind="dma")
    tg = mark("gpsimd")
    wait("vector", tg)
    wait("vector", t_dv)
    V(lambda e: e.tensor_tensor(prodf[0:1, :], prodf[0:1, :], dupvt[0:1, :], Alu.mult))
    V(lambda e: e.tensor_reduce(arowf[0:1, :],
                                prodf[0:1, :].rearrange("p (s c) -> p s c", c=C2),
                                AX.X, Alu.add))
    tv = mark("vector")
    wait("gpsimd", tv)
    t_w1 = stw(asc_s, arowf[0:1, :])
    wait("sync", t_w1)
    t_r = ld(arept, asc_s.rearrange("o (a p) -> (o p) a", p=128), slow=True)
    wait("vector", t_r)
    V(lambda e: e.tensor_tensor(araw, araw, arept, Alu.add))
    fence("vector")
    V(lambda e: e.tensor_reduce(small[:, 0:1], araw, AX.X, Alu.max))
    tv = mark("vector")
    wait("tensor", tv)
    pb = bank()
    T(lambda e, pb=pb: e.transpose(pb[0:1, 0:128], small[:, 0:1], ident))
    tmm = mark("tensor")
    wait("vector", tmm)
    V(lambda e, pb=pb: e.tensor_copy(srow[:, 0, 0:128], pb[0:1, 0:128]))
    fence("vector")
    V(lambda e: e.tensor_reduce(srow[:, 0, 128:129], srow[:, 0, 0:128], AX.X, Alu.max))
    tv = mark("vector")
    wait("tensor", tv)
    T(lambda e: e.matmul(ps[4][:, 0:1], onesrow, srow[:, 0, 128:129], start=True, stop=True))
    tmm = mark("tensor")
    wait("scalar", tmm)
    S(lambda e: e.activation(small[:, 2:3], ps[4][:, 0:1], AF.Identity, scale=-1.0))
    ts = mark("scalar")
    wait("scalar", ts)
    S(lambda e: e.activation(araw, araw, AF.Exp, bias=small[:, 2:3]))
    ts = mark("scalar")
    wait("vector", ts)
    V(lambda e: e.tensor_reduce(small[:, 3:4], araw, AX.X, Alu.add))
    tv = mark("vector")
    wait("tensor", tv)
    pb = bank()
    T(lambda e, pb=pb: e.transpose(pb[0:1, 0:128], small[:, 3:4], ident))
    tmm = mark("tensor")
    wait("vector", tmm)
    V(lambda e, pb=pb: e.tensor_copy(srow[:, 0, 0:128], pb[0:1, 0:128]))
    fence("vector")
    V(lambda e: e.tensor_reduce(srow[:, 0, 129:130], srow[:, 0, 0:128], AX.X, Alu.add))
    fence("vector")
    V(lambda e: e.reciprocal(srow[:, 0, 130:131], srow[:, 0, 129:130]))
    tv = mark("vector")
    wait("tensor", tv)
    T(lambda e: e.matmul(ps[5][:, 0:1], onesrow, srow[:, 0, 130:131], start=True, stop=True))
    tmm = mark("tensor")
    wait("vector", tmm)
    V(lambda e: e.tensor_scalar(araw, araw, ps[5][:, 0:1], None, Alu.mult))
    tv = mark("vector")
    wait("gpsimd", tv)
    t_w2 = stw(asc_s.rearrange("o (a p) -> (o p) a", p=128), araw, slow=True)
    wait("sync", t_w2)
    t_r = ld(arowf[0:1, :], asc_s)
    alpha_full = f32s(0, SEQ)
    wait("tensor", t_r)
    for tl in range(8):
        T(lambda e, tl=tl: e.matmul(ps[4], onesrow, arowf[0:1, SLAB * tl:SLAB * tl + SLAB],
                                    start=True, stop=True))
        tmm = mark("tensor")
        wait("vector", tmm)
        V(lambda e, tl=tl: e.tensor_copy(alpha_full[:, SLAB * tl:SLAB * tl + SLAB], ps[4]))
        tv = mark("vector")
        wait("tensor", tv)
    tv = mark("vector")
    wait("gpsimd", tv)
    p.on("gpsimd", lambda e: e.load_library(_APG_LIB), kind="dma")
    G(lambda e: e.ap_gather(alphaw, alpha_full, winmap, channels=128,
                            num_elems=SEQ, d=1, num_idxs=WIN))
    xg0stage = f32s(4096, SEQ)
    for k in range(KC):
        t_x = ld(xg0stage, fmv(xg0_s)[:, k, :])
        wait("gpsimd", t_x)
        G(lambda e, k=k: e.ap_gather(xg0w[:, k, :], xg0stage, winmap, channels=128,
                                     num_elems=SEQ, d=1, num_idxs=WIN))
        tg = mark("gpsimd")
        wait("sync", tg)
    p.on("gpsimd", lambda e: e.load_library(_STD_LIB), kind="dma")
    stw(dram["dbg_aw"], alphaw)
    stw(dram["dbg_xw"], xg0w)
    full_barrier()
    ckpt()

    wvt = bf16s(0, KC, F)
    xbsl2 = bf16s(8192, KC, SLAB)
    ybsl2 = bf16s(8192, KC, SLAB)
    sqsl2 = bf16s(12288, KC, SLAB)
    dbsl = bf16s(12288, KC, SLAB)
    msl2 = bf16s(12288, KC, SLAB)
    hbuf2 = bf16s(16384, SLAB)
    x1sl = f32s(0, KC, SLAB)
    dsl = f32s(4096, KC, SLAB)
    valf = f32s(8192, KC, SLAB)
    x2sl = f32s(12288, KC, SLAB)
    tmpa = f32s(16384, SLAB)
    tmpb = f32s(16896, SLAB)
    tmpc = f32s(17408, SLAB)
    xq2 = fp8s(0, KC, SLAB)
    hq2 = fp8s(4096, HC, SLAB)

    wstage5 = f32s(0, KC, F)
    t_w = ld(wstage5, fmv(wv_d))
    wait("vector", t_w)
    for k in range(KC):
        V(lambda e, k=k: e.tensor_copy(wvt[:, k, :], wstage5[:, k, :]))
    full_barrier()

    for sl in range(8):
        c0 = SLAB * sl
        t_x1 = ld(x1sl, fmv(x1s)[:, :, SEQ + c0:SEQ + c0 + SLAB])
        wait("vector", t_x1)
        wait("scalar", t_x1)
        emit_ln(x1sl, xbsl2, sqsl2, tmpa, out_f32=dsl, out_bf=dbsl)
        tv = mark("vector")
        wait("tensor", tv)
        if sl == 0:
            wait("gpsimd", tv)
            stw(dram["dbg_dsl"], dsl)
        for fo in range(KC):
            pb = bank()
            for k in range(KC):
                T(lambda e, pb=pb, fo=fo, k=k: e.matmul(
                    pb, wvt[:, k, 128 * fo:128 * fo + 128], dbsl[:, k, :],
                    start=(k == 0), stop=(k == KC - 1)))
            tmm = mark("tensor")
            wait("scalar", tmm)
            S(lambda e, pb=pb, fo=fo: e.activation(valf[:, fo, :], pb, AF.Identity,
                                                   bias=wvb[:, fo:fo + 1]))
            ts = mark("scalar")
            wait("tensor", ts)
        t_val = mark("scalar")
        wait("vector", t_val)
        if sl == 0:
            wait("gpsimd", t_val)
            stw(dram["dbg_val"], valf)
        for k in range(KC):
            V(lambda e, k=k: e.tensor_tensor(tmpa, valf[:, k, :], alphaw, Alu.mult))
            tv1 = mark("vector")
            wait("scalar", tv1)
            S(lambda e: e.activation(tmpb, tmpa, AF.Sigmoid, scale=-1.0))
            S(lambda e: e.activation(tmpb, tmpb, AF.Square))
            ts1 = mark("scalar")
            wait("vector", ts1)
            V(lambda e: e.tensor_scalar(tmpb, tmpb, 2.0, None, Alu.mult))
            V(lambda e, k=k: e.tensor_tensor(tmpc, xg0w[:, k, :], dsl[:, k, :], Alu.add))
            V(lambda e: e.tensor_tensor(tmpa, tmpa, tmpb, Alu.mult))
            V(lambda e: e.tensor_scalar(tmpb, tmpb, -1.0, 2.0, Alu.mult, Alu.add))
            V(lambda e: e.tensor_tensor(tmpc, tmpc, tmpb, Alu.mult))
            V(lambda e: e.tensor_tensor(tmpa, tmpa, tmpc, Alu.add))
            V(lambda e, k=k: e.tensor_tensor(x2sl[:, k, :], x1sl[:, k, :], tmpa, Alu.add))
        tv = mark("vector")
        wait("scalar", tv)
        if sl == 0:
            wait("gpsimd", tv)
            stw(dram["dbg_x2"], x2sl)
        emit_ln(x2sl, xbsl2, sqsl2, tmpa, out_f8=xq2)
        tv = mark("vector")
        wait("tensor", tv)
        wait("sync", tv)
        t_m = ld(msl2, fmv(mask_own)[:, :, c0:c0 + SLAB])
        emit_ffn(xq2, hbuf2, hq2, ybsl2)
        wait("vector", mark("scalar"))
        wait("vector", t_m)
        for k in range(KC):
            V(lambda e, k=k: e.tensor_tensor(tmpa, ybsl2[:, k, :], msl2[:, k, :], Alu.mult))
            V(lambda e, k=k: e.tensor_tensor(x2sl[:, k, :], x2sl[:, k, :], tmpa, Alu.add))
        tv = mark("vector")
        wait("gpsimd", tv)
        stw(fmv(yout)[:, :, c0:c0 + SLAB], x2sl)
        full_barrier()

    if upto is not None:
        lens, cnts, atts = _ckpts[upto]
        for e in p.ops:
            del p.ops[e][lens[e]:]
            del p.kinds[e][lens[e]:]
            p.attach[e] = {i: v[:atts[e][i]] for i, v in p.attach[e].items()
                           if i in atts[e]}
        p.cnt.clear()
        p.cnt.update(cnts)
        zsrc = f32s(0, 1, NT)
        for k in range(KC):
            stw(fmv(yout)[:, k:k + 1, :], zsrc)
        svv = p.val(s_sp)
        for e in ALL:
            p.on(e, lambda en, svv=svv: en.wait_ge(s_sp, svv), kind="wait")

    lv, sv = p.val(s_ld), p.val(s_sp)
    p.on("gpsimd", lambda e: e.wait_ge(s_sp, sv), kind="wait")
    p.on("sync", lambda e: e.wait_ge(s_ld, lv), kind="wait")

    with nc.Block() as block:
        @block.sync
        def _(sync):
            p.replay("sync", sync)

        @block.gpsimd
        def _(g):
            p.replay("gpsimd", g)

        @block.scalar
        def _(s):
            p.replay("scalar", s)

        @block.vector
        def _(v):
            p.replay("vector", v)

        @block.tensor
        def _(t):
            p.replay("tensor", t)

    for cm in reversed(_sem_keep):
        cm.__exit__(None, None, None)
    mybir.codegen_inst_isa_subclasses(nc)
    return nc


def kernel(L0_attn_dense, L0_attn_adj_rows, L0_attn_adj_cols, L0_attn_adj_vals,
           L0_attn_wq, L0_attn_wk, L0_attn_wv_w, L0_attn_wv_b, L0_attn_qkv_w,
           L0_attn_out_w, L0_attn_out_b, L0_attn_conv_w,
           L0_ffn_W1, L0_ffn_b1, L0_ffn_W2, L0_ffn_b2):
    x = np.ascontiguousarray(np.asarray(L0_attn_dense, np.float32))
    S, clampv, C1, dupc_flat, dupv, C2 = _prep_sparse(
        L0_attn_adj_rows, L0_attn_adj_cols, L0_attn_adj_vals)
    mask = _dropout_mask()
    maskv = np.where(mask, np.float32(0.625), np.float32(0.0)).astype(NPBF16)

    nc = build_nc(C1, C2)

    common = {
        "W1": np.ascontiguousarray(np.asarray(L0_ffn_W1, np.float32)),
        "W2": np.ascontiguousarray(np.asarray(L0_ffn_W2, np.float32)),
        "b1v": _fold_bias(L0_ffn_b1, HC),
        "b2v": _fold_bias(L0_ffn_b2, KC),
        "wvb": _fold_bias(L0_attn_wv_b, KC),
        "outb": _fold_bias(L0_attn_out_b, KC),
        "wq": np.ascontiguousarray(np.asarray(L0_attn_wq, np.float32)),
        "wk": np.ascontiguousarray(np.asarray(L0_attn_wk, np.float32)),
        "wv": np.ascontiguousarray(np.asarray(L0_attn_wv_w, np.float32)),
        "qkvw": np.ascontiguousarray(np.asarray(L0_attn_qkv_w, np.float32)),
        "outw": np.ascontiguousarray(np.asarray(L0_attn_out_w, np.float32)),
        "convT": _conv_toeplitz(L0_attn_conv_w),
        "Sd": S,
        "clampv": clampv,
        "dupc": _interleave_idx(dupc_flat),
        "dupv": dupv,
        "xb0": _fm(x[0]),
        "mask_b0": np.ascontiguousarray(maskv[0].T),
        "ident": np.eye(128, dtype=np.float32),
        "onesrow": np.ones((1, 128), np.float32),
        "onescol": np.ones((128, 1), np.float32),
    }
    in_maps = []
    for c in range(8):
        w0, w1 = WIN * c, WIN * c + WIN
        m = dict(common)
        m["xown"] = np.ascontiguousarray(x[:, w0:w1, :].reshape(8 * WIN, F).T)
        m["mask_own"] = np.ascontiguousarray(
            maskv[:, w0:w1, :].reshape(8 * WIN, F).T)
        m["winmap"] = _interleave_idx(np.arange(w0, w1))
        in_maps.append(m)

    res = run_bass_kernel_spmd(nc, in_maps, core_ids=list(range(8)))
    out = np.empty((8, SEQ, F), np.float32)
    for c in range(8):
        yo = np.asarray(res.results[c]["yout"])
        out[:, WIN * c:WIN * c + WIN, :] = yo.T.reshape(8, WIN, F)
    return out


# version 07 (final)
# speedup vs baseline: 1.0004x; 1.0004x over previous
import numpy as np
import ml_dtypes
import concourse.bass as bass
import concourse.mybir as mybir
from concourse.bass_utils import run_bass_kernel_spmd
from concourse.library_config import ap_gather as _APG_LIB, standard as _STD_LIB

F32 = mybir.dt.float32
BF16 = mybir.dt.bfloat16
FP8 = mybir.dt.float8e5
I16 = mybir.dt.int16
Alu = mybir.AluOpType
AF = mybir.ActivationFunctionType
AX = mybir.AxisListType
NPBF16 = ml_dtypes.bfloat16

F = 1024
KC = 8
SEQ = 4096
WIN = 512
NT = 4096
HID = 4096
HC = 32
SLAB = 512
NHEAD = 8
NST = 32


def _threefry2x32(k1, k2, x1, x2):
    def rotl(v, d):
        return ((v << np.uint32(d)) | (v >> np.uint32(32 - d))).astype(np.uint32)
    ks = [np.uint32(k1), np.uint32(k2),
          np.uint32(k1) ^ np.uint32(k2) ^ np.uint32(0x1BD11BDA)]
    x = [(x1 + ks[0]).astype(np.uint32), (x2 + ks[1]).astype(np.uint32)]
    rot = [[13, 15, 26, 6], [17, 29, 16, 24]]
    inj = [(1, 2, 1), (2, 0, 2), (0, 1, 3), (1, 2, 4), (2, 0, 5)]
    for grp in range(5):
        for d in rot[grp % 2]:
            x[0] = (x[0] + x[1]).astype(np.uint32)
            x[1] = rotl(x[1], d)
            x[1] = x[0] ^ x[1]
        a, b, c = inj[grp]
        x[0] = (x[0] + ks[a]).astype(np.uint32)
        x[1] = (x[1] + ks[b] + np.uint32(c)).astype(np.uint32)
    return x[0], x[1]


def _dropout_mask():
    n = 8 * SEQ * F
    iota = np.arange(n, dtype=np.uint64)
    c1 = (iota >> np.uint64(32)).astype(np.uint32)
    c2 = (iota & np.uint64(0xFFFFFFFF)).astype(np.uint32)
    b1, b2 = _threefry2x32(0, 42, c1, c2)
    bits = (b1 ^ b2).astype(np.uint32)
    u = ((bits >> np.uint32(9)) | np.uint32(0x3F800000)).view(np.float32) - np.float32(1.0)
    return (u < np.float32(0.8)).reshape(8, SEQ, F)


def _interleave_idx(idx):
    n = len(idx)
    assert n % 16 == 0
    out = np.zeros((128, n // 16), np.int16)
    blk = np.asarray(idx, np.int64).reshape(n // 16, 16).T.astype(np.int16)
    for g in range(8):
        out[16 * g:16 * g + 16, :] = blk
    return np.ascontiguousarray(out)


def _prep_sparse(rows, cols, vals):
    rows = np.asarray(rows).astype(np.int64)
    cols = np.asarray(cols).astype(np.int64)
    vals = np.asarray(vals, np.float32)
    inb = rows < SEQ
    r, c, v = rows[inb], cols[inb], vals[inb]
    clamped = c >= SEQ
    rc, vc = r[clamped], v[clamped]
    cnt = np.bincount(rc, minlength=SEQ) if rc.size else np.zeros(SEQ, np.int64)
    C1 = max(int(cnt.max()) if rc.size else 0, 1)
    clampv = np.zeros((SEQ, C1), np.float32)
    off = np.zeros(SEQ, np.int64)
    for i in range(rc.shape[0]):
        rr = rc[i]
        clampv[rr, off[rr]] = vc[i]
        off[rr] += 1
    ri, ci, vi = r[~clamped], c[~clamped], v[~clamped]
    key = ri * SEQ + ci
    order = np.argsort(key, kind="stable")
    ri, ci, vi, key = ri[order], ci[order], vi[order], key[order]
    first = np.ones(key.shape[0], bool)
    if key.shape[0] > 1:
        first[1:] = key[1:] != key[:-1]
    S = np.zeros((SEQ, SEQ), np.float32)
    S[ri[first], ci[first]] = vi[first]
    rd, cd, vd = ri[~first], ci[~first], vi[~first]
    cnt2 = np.bincount(rd, minlength=SEQ) if rd.size else np.zeros(SEQ, np.int64)
    C2 = max(int(cnt2.max()) if rd.size else 0, 1)
    dupc = np.zeros((SEQ, C2), np.int64)
    dupv = np.zeros((SEQ, C2), np.float32)
    off2 = np.zeros(SEQ, np.int64)
    for i in range(rd.shape[0]):
        rr = rd[i]
        dupc[rr, off2[rr]] = cd[i]
        dupv[rr, off2[rr]] = vd[i]
        off2[rr] += 1
    cl = clampv.reshape(NST, 128, C1).transpose(1, 0, 2).reshape(128, NST * C1)
    return (np.ascontiguousarray(S), np.ascontiguousarray(cl), C1,
            dupc.reshape(-1), np.ascontiguousarray(dupv.reshape(1, -1)), C2)


def _conv_toeplitz(conv_w):
    w = np.asarray(conv_w, np.float32)[:, 0, :, 0]
    T = np.zeros((NHEAD, 3, 128, 128), np.float32)
    ii = np.arange(128)[:, None]
    jj = np.arange(128)[None, :]
    for d, dlt in ((0, -128), (1, 0), (2, 128)):
        t = (ii + dlt) - jj + 16
        m = (t >= 0) & (t < 33)
        tm = np.where(m, t, 0)
        for h in range(NHEAD):
            T[h, d] = np.where(m, w[h][tm], 0.0)
    return np.ascontiguousarray(T.transpose(2, 0, 1, 3).reshape(128, NHEAD * 3 * 128))


def _fm(a):
    return np.ascontiguousarray(np.asarray(a, np.float32).T)


def _fold_bias(b, nchunk):
    return np.ascontiguousarray(np.asarray(b, np.float32).reshape(nchunk, 128).T)


class Prog:

    def __init__(self):
        self.ops = {e: [] for e in ("sync", "gpsimd", "scalar", "vector", "tensor")}
        self.kinds = {e: [] for e in self.ops}
        self.attach = {e: {} for e in self.ops}
        self.cnt = {}

    def on(self, eng, fn, kind="op"):
        self.ops[eng].append(fn)
        self.kinds[eng].append(kind)

    def mark_attach(self, eng, sem, n):
        ks = self.kinds[eng]
        for j in range(len(ks) - 1, -1, -1):
            if ks[j] == "op":
                lst = self.attach[eng].setdefault(j, [])
                for i, (sem0, n0) in enumerate(lst):
                    if sem0 is sem:
                        lst[i] = (sem0, n0 + n)
                        return True
                if lst:
                    return False
                lst.append((sem, n))
                return True
            if ks[j] == "wait":
                continue
            return False
        return False

    def replay(self, eng, e):
        att = self.attach[eng]
        for i, fn in enumerate(self.ops[eng]):
            r = fn(e)
            for sem_, n in att.get(i, ()):
                try:
                    r.then_inc(sem_, n)
                except Exception:
                    e.sem_inc(sem_, n)

    def bump(self, sem, n):
        self.cnt[sem.name] = self.cnt.get(sem.name, 0) + n
        return self.cnt[sem.name]

    def val(self, sem):
        return self.cnt.get(sem.name, 0)


def build_nc(C1, C2, upto=None, dbg_ext=False):
    nc = bass.Bass("TRN2", target_bir_lowering=False, debug=False,
                   detect_race_conditions=False)
    dram = {}

    def din(name, shape, dt=F32):
        dram[name] = nc.dram_tensor(name, shape, dt, kind="ExternalInput").ap()
        return dram[name]

    def dint(name, shape, dt=F32):
        kind = "ExternalOutput" if dbg_ext else "Internal"
        dram[name] = nc.dram_tensor(name, shape, dt, kind=kind).ap()
        return dram[name]

    xb0 = din("xb0", [F, SEQ])
    xown = din("xown", [F, NT])
    mask_b0 = din("mask_b0", [F, SEQ], BF16)
    mask_own = din("mask_own", [F, NT], BF16)
    W1_d = din("W1", [HID, F])
    W2_d = din("W2", [F, HID])
    b1v = din("b1v", [128, HC])
    b2v = din("b2v", [128, KC])
    wvb_d = din("wvb", [128, KC])
    outb_d = din("outb", [128, KC])
    wq_d = din("wq", [F, 256])
    wk_d = din("wk", [F, 256])
    wv_d = din("wv", [F, F])
    qkv_d = din("qkvw", [F, 1536])
    outw_d = din("outw", [512, F])
    convT_d = din("convT", [128, NHEAD * 3 * 128])
    Sd = din("Sd", [SEQ, SEQ])
    clampv_d = din("clampv", [128, NST * C1])
    dupc_d = din("dupc", [128, (SEQ * C2) // 16], I16)
    dupv_d = din("dupv", [1, SEQ * C2])
    winmap_d = din("winmap", [128, WIN // 16], I16)
    ident_d = din("ident", [128, 128])
    onesrow_d = din("onesrow", [1, 128])
    onescol_d = din("onescol", [128, 1])
    yout = nc.dram_tensor("yout", [F, NT], F32, kind="ExternalOutput").ap()

    x1s = dint("x1s", [F, SEQ + NT])
    d0b_s = dint("d0b", [F, SEQ], BF16)
    d0f_s = dint("d0f", [F, SEQ])
    qkv_s = dint("qkvs", [1536, SEQ], BF16)
    vsm_s = dint("vsms", [SEQ, 512], BF16)
    xg0_s = dint("xg0s", [F, SEQ])
    asc_s = dint("asc", [1, SEQ])
    w2qs = dint("w2qs", [HID, F], FP8)
    dbg_xq = dint("dbg_xq", [128, KC, SLAB], FP8)
    dbg_hq = dint("dbg_hq", [128, HC, SLAB], FP8)
    dbg_yb = dint("dbg_yb", [128, KC, SLAB], BF16)
    dbg_w1 = dint("dbg_w1", [128, KC, HID], FP8)
    dbg_ms = dint("dbg_ms", [128, KC, SLAB], BF16)
    dbg_x1 = dint("dbg_x1", [128, KC, SLAB])
    dbg_qw = dint("dbg_qw", [128, KC, 1536], BF16)
    dbg_d0s = dint("dbg_d0s", [128, KC, SLAB], BF16)
    dbg_aw = dint("dbg_aw", [128, WIN])
    dbg_xw = dint("dbg_xw", [128, KC, WIN])
    dbg_dsl = dint("dbg_dsl", [128, KC, SLAB])
    dbg_val = dint("dbg_val", [128, KC, SLAB])
    dbg_x2 = dint("dbg_x2", [128, KC, SLAB])

    def sbt(name, shape, dt=F32):
        return nc.alloc_sbuf_tensor(name, shape, dt).ap()

    w1qT = sbt("w1qT", [128, KC, HID], FP8)
    ident = sbt("identt", [128, 128])
    identb = sbt("identb", [128, 128], BF16)
    onesrow = sbt("onesrowt", [1, 128])
    onescol = sbt("onescolt", [128, 1])
    onescol_b = sbt("onescol_b", [128, 1], BF16)
    b1q = sbt("b1q", [128, HC])
    b2q = sbt("b2q", [128, KC])
    wvb = sbt("wvbt", [128, KC])
    outb = sbt("outbt", [128, KC])
    convT = sbt("convTt", [128, NHEAD, 3, 128], BF16)
    winmap = sbt("winmapt", [128, WIN // 16], I16)
    alphaw = sbt("alphawt", [128, WIN])
    xg0w = sbt("xg0wt", [128, KC, WIN])
    small = sbt("small", [128, 16])
    srow = sbt("srow", [1, 4, SLAB])
    qlkl = sbt("qlkl", [128, 8, 256], BF16)
    t3b = sbt("t3b", [128, NHEAD, 2, 64], BF16)
    araw = sbt("araw", [128, NST])
    trf = sbt("trf", [128, 128])
    dupct = sbt("dupctt", [128, (SEQ * C2) // 16], I16)

    NAF, NAB, NAQ = 18432, 17408, 22528
    AFT = sbt("AF32", [128, NAF])
    ABT = sbt("ABF", [128, NAB], BF16)
    AQT = sbt("AF8", [128, NAQ], FP8)

    def f32s(off, a, b=None):
        if b is None:
            assert off + a <= NAF
            return AFT[:, off:off + a]
        assert off + a * b <= NAF
        return AFT[:, off:off + a * b].rearrange("p (a b) -> p a b", a=a)

    def bf16s(off, a, b=None):
        if b is None:
            assert off + a <= NAB
            return ABT[:, off:off + a]
        assert off + a * b <= NAB
        return ABT[:, off:off + a * b].rearrange("p (a b) -> p a b", a=a)

    def fp8s(off, a, b=None):
        if b is None:
            assert off + a <= NAQ
            return AQT[:, off:off + a]
        assert off + a * b <= NAQ
        return AQT[:, off:off + a * b].rearrange("p (a b) -> p a b", a=a)

    ps = [nc.alloc_psum_tensor(f"ps{i}", [128, SLAB], F32).ap() for i in range(8)]

    p = Prog()
    _sem_keep = []

    def mksem(name):
        cm = nc.semaphore(name)
        h = cm.__enter__()
        _sem_keep.append(cm)
        return h

    s_ld = mksem("s_ld")
    s_sp = mksem("s_sp")
    s_t = mksem("s_t")
    s_v = mksem("s_v")
    s_s = mksem("s_s")
    s_g = mksem("s_g")
    ENG = {"tensor": s_t, "vector": s_v, "scalar": s_s, "gpsimd": s_g}

    def ld(dst, src, slow=False):
        v = p.bump(s_ld, 16)
        p.on("sync", lambda e, dst=dst, src=src, slow=slow: e.dma_start(
            out=dst, in_=src, allow_slow_non_contiguous=slow).then_inc(s_ld, 16),
             kind="dma")
        return (s_ld, v)

    def stw(dst, src, slow=False):
        v = p.bump(s_sp, 16)
        p.on("gpsimd", lambda e, dst=dst, src=src, slow=slow: e.dma_start(
            out=dst, in_=src, allow_slow_non_contiguous=slow).then_inc(s_sp, 16),
             kind="dma")
        return (s_sp, v)

    def wait(eng, tok):
        if tok is None:
            return
        sem_, v = tok
        p.on(eng, lambda e, sem_=sem_, v=v: e.wait_ge(sem_, v), kind="wait")

    def mark(eng):
        sem_ = ENG[eng]
        v = p.bump(sem_, 1)
        if not p.mark_attach(eng, sem_, 1):
            p.on(eng, lambda e, sem_=sem_: e.sem_inc(sem_, 1), kind="wait")
        return (sem_, v)

    def fence(eng):
        wait(eng, mark(eng))

    ALL = ("sync", "gpsimd", "scalar", "vector", "tensor")

    def full_barrier():
        toks = [mark(e) for e in ("gpsimd", "scalar", "vector", "tensor")]
        lv, sv = p.val(s_ld), p.val(s_sp)
        for c in ALL:
            for t in toks:
                if t[0] is not ENG.get(c):
                    wait(c, t)
            p.on(c, lambda e, lv=lv: e.wait_ge(s_ld, lv), kind="wait")
            p.on(c, lambda e, sv=sv: e.wait_ge(s_sp, sv), kind="wait")

    _ckpts = []

    def ckpt():
        _ckpts.append(({e: len(p.ops[e]) for e in p.ops}, dict(p.cnt),
                       {e: {i: len(v) for i, v in p.attach[e].items()}
                        for e in p.ops}))

    _bank = [0]

    def bank():
        b = ps[_bank[0] % 4]
        _bank[0] += 1
        return b

    def V(fn):
        p.on("vector", fn)

    def S(fn):
        p.on("scalar", fn)

    def T(fn):
        p.on("tensor", fn)

    def G(fn):
        p.on("gpsimd", fn)

    def fmv(d):
        return d.rearrange("(a p) s -> p a s", p=128)

    def mmgroup(items):
        n = len(items)
        for i, (pb, lhsT, rhs) in enumerate(items):
            T(lambda e, pb=pb, lhsT=lhsT, rhs=rhs, st=(i == 0), sp=(i == n - 1):
              e.matmul(pb, lhsT, rhs, start=st, stop=sp))

    ld(ident, ident_d)
    ld(onesrow, onesrow_d)
    ld(onescol, onescol_d)
    ld(b1q, b1v)
    ld(b2q, b2v)
    ld(wvb, wvb_d)
    ld(outb, outb_d)
    ld(winmap, winmap_d)
    ld(dupct, dupc_d)
    cstage = f32s(0, NHEAD * 3 * 128)
    t_conv = ld(cstage, convT_d)
    wait("vector", t_conv)
    V(lambda e: e.tensor_copy(identb, ident))
    V(lambda e: e.tensor_copy(onescol_b, onescol))
    V(lambda e: e.tensor_copy(convT.rearrange("p h t q -> p (h t q)"), cstage))
    bq8 = fp8s(0, HC)
    V(lambda e: e.tensor_copy(bq8, b1q))
    fence("vector")
    V(lambda e: e.tensor_copy(b1q, bq8))
    bq8b = fp8s(HC, KC)
    V(lambda e: e.tensor_copy(bq8b, b2q))
    fence("vector")
    V(lambda e: e.tensor_copy(b2q, bq8b))
    full_barrier()

    wstage = f32s(0, 2, F)
    tok_use = None
    for ncx in range(HC):
        buf = wstage[:, ncx % 2, :]
        wait("sync", tok_use)
        t_l = ld(buf, W1_d[128 * ncx:128 * ncx + 128, :])
        wait("tensor", t_l)
        for kc in range(KC):
            pb = bank()
            T(lambda e, pb=pb, buf=buf, kc=kc: e.transpose(
                pb[:, 0:128], buf[:, 128 * kc:128 * kc + 128], ident))
            tmm = mark("tensor")
            wait("vector", tmm)
            dst = w1qT[:, kc, 128 * ncx:128 * ncx + 128]
            V(lambda e, dst=dst, pb=pb: e.tensor_copy(dst, pb[:, 0:128]))
            tv = mark("vector")
            wait("tensor", tv)
        tok_use = mark("vector")
    w2row = fp8s(0, F)
    for ncx in range(HC):
        buf = wstage[:, ncx % 2, :]
        wait("sync", tok_use)
        t_l = ld(buf.rearrange("p (a b) -> p a b", a=KC),
                 W2_d[:, 128 * ncx:128 * ncx + 128].rearrange("(a p) n -> p a n", p=128))
        wait("tensor", t_l)
        wait("vector", (s_sp, p.val(s_sp)))
        for oc in range(KC):
            pb = bank()
            T(lambda e, pb=pb, buf=buf, oc=oc: e.transpose(
                pb[:, 0:128], buf[:, 128 * oc:128 * oc + 128], ident))
            tmm = mark("tensor")
            wait("vector", tmm)
            V(lambda e, pb=pb, oc=oc: e.tensor_copy(w2row[:, 128 * oc:128 * oc + 128],
                                                    pb[:, 0:128]))
            tv = mark("vector")
            wait("tensor", tv)
        tv = mark("vector")
        wait("gpsimd", tv)
        stw(w2qs[128 * ncx:128 * ncx + 128, :], w2row)
        tok_use = mark("vector")
    full_barrier()
    ckpt()

    def emit_ln(xsl, xbsl, sqsl, tmp, out_f32=None, out_bf=None, out_f8=None,
                stats_only=False):
        for k in range(KC):
            V(lambda e, k=k: e.tensor_copy(xbsl[:, k, :], xsl[:, k, :]))
            S(lambda e, k=k: e.activation(sqsl[:, k, :], xsl[:, k, :], AF.Square))
        tv = mark("vector")
        tsx = mark("scalar")
        wait("tensor", tv)
        wait("tensor", tsx)
        for k in range(KC):
            T(lambda e, k=k: e.matmul(ps[4][0:1, :], onescol_b, xbsl[:, k, :],
                                      start=(k == 0), stop=(k == KC - 1)))
        for k in range(KC):
            T(lambda e, k=k: e.matmul(ps[5][0:1, :], onescol_b, sqsl[:, k, :],
                                      start=(k == 0), stop=(k == KC - 1)))
        tmm = mark("tensor")
        wait("vector", tmm)
        mr, rr, vv, zz = srow[:, 0, :], srow[:, 1, :], srow[:, 2, :], srow[:, 3, :]
        V(lambda e: e.tensor_scalar(mr, ps[4][0:1, :], 1.0 / F, None, Alu.mult))
        V(lambda e: e.tensor_scalar(vv, ps[5][0:1, :], 1.0 / F, None, Alu.mult))
        V(lambda e: e.tensor_tensor(zz, mr, mr, Alu.mult))
        V(lambda e: e.tensor_tensor(vv, vv, zz, Alu.subtract))
        V(lambda e: e.tensor_scalar(vv, vv, 1e-5, None, Alu.add))
        tv2 = mark("vector")
        wait("scalar", tv2)
        S(lambda e: e.activation(zz, vv, AF.Sqrt))
        ts2 = mark("scalar")
        wait("vector", ts2)
        V(lambda e: e.reciprocal(rr, zz))
        tv3 = mark("vector")
        wait("tensor", tv3)
        T(lambda e: e.matmul(ps[6], onesrow, mr, start=True, stop=True))
        T(lambda e: e.matmul(ps[7], onesrow, rr, start=True, stop=True))
        tb = mark("tensor")
        wait("vector", tb)
        if stats_only:
            return
        for k in range(KC):
            V(lambda e, k=k, tmp=tmp: e.tensor_tensor(tmp, xsl[:, k, :], ps[6], Alu.subtract))
            if out_f32 is not None:
                V(lambda e, k=k, tmp=tmp: e.tensor_tensor(out_f32[:, k, :], tmp, ps[7], Alu.mult))
                if out_bf is not None:
                    V(lambda e, k=k: e.tensor_copy(out_bf[:, k, :], out_f32[:, k, :]))
                if out_f8 is not None:
                    V(lambda e, k=k: e.tensor_copy(out_f8[:, k, :], out_f32[:, k, :]))
            elif out_bf is not None:
                V(lambda e, k=k, tmp=tmp: e.tensor_tensor(out_bf[:, k, :], tmp, ps[7], Alu.mult))
                if out_f8 is not None:
                    V(lambda e, k=k: e.tensor_copy(out_f8[:, k, :], out_bf[:, k, :]))
            else:
                V(lambda e, k=k, tmp=tmp: e.tensor_tensor(out_f8[:, k, :], tmp, ps[7], Alu.mult))

    def emit_ffn(xq, hbuf, hq, ybsl):
        toks = []
        for ncx in range(HC):
            if ncx >= 3:
                wait("tensor", toks[ncx - 3])
            pb = bank()
            for k in range(KC):
                T(lambda e, pb=pb, ncx=ncx, k=k: e.matmul(
                    pb, w1qT[:, k, 128 * ncx:128 * ncx + 128], xq[:, k, :],
                    start=(k == 0), stop=(k == KC - 1)))
            tmm = mark("tensor")
            wait("scalar", tmm)
            bias = b1q[:, ncx:ncx + 1]
            S(lambda e, pb=pb, bias=bias, hbuf=hbuf: e.activation(
                hbuf, pb, AF.Identity, bias=bias))
            S(lambda e, ncx=ncx, hbuf=hbuf, hq=hq: e.activation(
                hq[:, ncx, :], hbuf, AF.Relu))
            toks.append(mark("scalar"))
        wait("tensor", toks[-1])
        donetoks = []
        for ncx in range(HC):
            wbuf = w2sb[:, ncx % 2, :]
            if ncx >= 2:
                wait("sync", donetoks[ncx - 2])
            tok_l = ld(wbuf, w2qs[128 * ncx:128 * ncx + 128, :])
            wait("tensor", tok_l)
            for ocx in range(KC):
                T(lambda e, ocx=ocx, ncx=ncx, wbuf=wbuf: e.matmul(
                    ps[ocx], wbuf[:, 128 * ocx:128 * ocx + 128], hq[:, ncx, :],
                    start=(ncx == 0), stop=(ncx == HC - 1)))
            donetoks.append(mark("tensor"))
        tmm = mark("tensor")
        wait("scalar", tmm)
        for ocx in range(KC):
            bias = b2q[:, ocx:ocx + 1]
            S(lambda e, ocx=ocx, bias=bias, ybsl=ybsl: e.activation(
                ybsl[:, ocx, :], ps[ocx], AF.Identity, bias=bias))
        ts_f = mark("scalar")
        wait("tensor", ts_f)

    def soft_rows(bufs, outs):
        nb = len(bufs)
        for i, b in enumerate(bufs):
            V(lambda e, b=b, i=i: e.tensor_reduce(small[:, i:i + 1], b, AX.X,
                                                  Alu.max, negate=True))
        tv = mark("vector")
        wait("scalar", tv)
        for i, b in enumerate(bufs):
            S(lambda e, b=b, i=i: e.activation(b, b, AF.Exp, bias=small[:, i:i + 1]))
        ts = mark("scalar")
        wait("vector", ts)
        for i, b in enumerate(bufs):
            V(lambda e, b=b, i=i: e.tensor_reduce(small[:, 8 + i:9 + i], b, AX.X, Alu.add))
        fence("vector")
        V(lambda e, nb=nb: e.reciprocal(small[:, 8:8 + nb], small[:, 8:8 + nb]))
        fence("vector")
        for i, (b, o) in enumerate(zip(bufs, outs)):
            V(lambda e, b=b, o=o, i=i: e.tensor_scalar(o, b, small[:, 8 + i:9 + i],
                                                       None, Alu.mult))

    def tr128(dst_slice, src_bf_tile):
        V(lambda e, src_bf_tile=src_bf_tile: e.tensor_copy(trf, src_bf_tile))
        tv = mark("vector")
        wait("tensor", tv)
        pb = bank()
        T(lambda e, pb=pb: e.transpose(pb[:, 0:128], trf, ident))
        tmm = mark("tensor")
        wait("vector", tmm)
        V(lambda e, pb=pb, dst_slice=dst_slice: e.tensor_copy(dst_slice, pb[:, 0:128]))
        tv = mark("vector")
        wait("tensor", tv)

    def transpose256(dstb, srcb):
        for ic in range(2):
            for jc in range(2):
                tr128(dstb[:, jc, 128 * ic:128 * ic + 128],
                      srcb[:, ic, 128 * jc:128 * jc + 128])

    def mm256(dst_f32, lhsTb, rhsb):
        for ic in range(2):
            pb = ps[ic]
            for kc2 in range(2):
                T(lambda e, pb=pb, ic=ic, kc2=kc2, lhsTb=lhsTb, rhsb=rhsb: e.matmul(
                    pb[:, 0:256], lhsTb[:, kc2, 128 * ic:128 * ic + 128],
                    rhsb[:, kc2, :], start=(kc2 == 0), stop=(kc2 == 1)))
        tmm = mark("tensor")
        wait("vector", tmm)
        for ic in range(2):
            V(lambda e, ic=ic, dst_f32=dst_f32: e.tensor_copy(dst_f32[:, ic, :],
                                                              ps[ic][:, 0:256]))
        tv = mark("vector")
        wait("tensor", tv)

    xslab = f32s(0, KC, SLAB)
    x1slab = f32s(4096, KC, SLAB)
    tmp1 = f32s(8192, SLAB)
    d0fs = f32s(8704, KC, SLAB)
    xbslab = bf16s(0, KC, SLAB)
    sqslab = bf16s(4096, KC, SLAB)
    mslab = bf16s(8192, KC, SLAB)
    ybslab = bf16s(12288, KC, SLAB)
    hbuf = bf16s(16384, SLAB)
    d0bs = bf16s(8192, KC, SLAB)
    xq = fp8s(0, KC, SLAB)
    hq = fp8s(4096, HC, SLAB)
    w2sb = fp8s(20480, 2, F)

    for sl in range(16):
        isb0 = sl < 8
        src = xb0 if isb0 else xown
        msrc = mask_b0 if isb0 else mask_own
        c0 = SLAB * (sl if isb0 else sl - 8)
        t_x = ld(xslab, fmv(src)[:, :, c0:c0 + SLAB])
        t_m = ld(mslab, fmv(msrc)[:, :, c0:c0 + SLAB])
        wait("vector", t_x)
        wait("scalar", t_x)
        emit_ln(xslab, xbslab, sqslab, tmp1, out_f8=xq)
        tv = mark("vector")
        wait("tensor", tv)
        emit_ffn(xq, hbuf, hq, ybslab)
        ts_dbg = mark("scalar")
        wait("vector", ts_dbg)
        wait("vector", t_m)
        for k in range(KC):
            V(lambda e, k=k: e.tensor_tensor(tmp1, ybslab[:, k, :], mslab[:, k, :], Alu.mult))
            V(lambda e, k=k: e.tensor_tensor(x1slab[:, k, :], xslab[:, k, :], tmp1, Alu.add))
        tv = mark("vector")
        wait("gpsimd", tv)
        stw(fmv(x1s)[:, :, sl * SLAB:sl * SLAB + SLAB], x1slab)
        if isb0:
            wait("scalar", tv)
            emit_ln(x1slab, xbslab, sqslab, tmp1, out_f32=d0fs, out_bf=d0bs)
            tv2 = mark("vector")
            wait("gpsimd", tv2)
            stw(fmv(d0f_s)[:, :, c0:c0 + SLAB], d0fs)
            stw(fmv(d0b_s)[:, :, c0:c0 + SLAB], d0bs)
        full_barrier()
    ckpt()

    qkvw_bf = bf16s(0, KC, 1536)
    wstage2 = f32s(0, KC, 1536)
    t_w = ld(wstage2, fmv(qkv_d))
    wait("vector", t_w)
    for k in range(KC):
        V(lambda e, k=k: e.tensor_copy(qkvw_bf[:, k, :], wstage2[:, k, :]))
    full_barrier()

    d0bstage = bf16s(12288, KC, SLAB)
    qtile = bf16s(16384, SLAB)
    vtile = bf16s(16896, 128)
    evacf = f32s(0, SLAB)
    lmacc = f32s(512, 8, 256)
    st_q = None
    st_v = None
    prev_t = None
    for stile in range(8):
        c0 = SLAB * stile
        wait("sync", prev_t)
        sv_now = p.val(s_sp)
        p.on("sync", lambda e, sv_now=sv_now: e.wait_ge(s_sp, sv_now),
             kind="wait")
        t_d = ld(d0bstage, fmv(d0b_s)[:, :, c0:c0 + SLAB])
        wait("tensor", t_d)
        for qf in range(12):
            pb = bank()
            for k in range(KC):
                T(lambda e, pb=pb, qf=qf, k=k: e.matmul(
                    pb, qkvw_bf[:, k, 128 * qf:128 * qf + 128], d0bstage[:, k, :],
                    start=(k == 0), stop=(k == KC - 1)))
            tmm = mark("tensor")
            wait("scalar", tmm)
            scale = 0.125 if qf < 4 else 1.0
            S(lambda e, pb=pb, scale=scale: e.activation(evacf, pb, AF.Identity, scale=scale))
            ts = mark("scalar")
            wait("vector", ts)
            wait("vector", st_q)
            V(lambda e: e.tensor_copy(qtile, evacf))
            if qf < 8:
                lm = lmacc[:, qf, 32 * stile:32 * stile + 32]
                V(lambda e, lm=lm: e.tensor_reduce(
                    lm, evacf.rearrange("p (a b) -> p a b", b=16), AX.X, Alu.add))
            tv = mark("vector")
            wait("gpsimd", tv)
            st_q = stw(qkv_s[128 * qf:128 * qf + 128, c0:c0 + SLAB], qtile)
            if qf >= 8:
                wait("tensor", ts)
                for tt in range(4):
                    pb2 = bank()
                    T(lambda e, pb2=pb2, tt=tt: e.transpose(
                        pb2[:, 0:128], evacf[:, 128 * tt:128 * tt + 128], ident))
                    tmm2 = mark("tensor")
                    wait("vector", tmm2)
                    wait("vector", st_v)
                    V(lambda e, pb2=pb2: e.tensor_copy(vtile, pb2[:, 0:128]))
                    tv2 = mark("vector")
                    wait("gpsimd", tv2)
                    st_v = stw(vsm_s[c0 + 128 * tt:c0 + 128 * tt + 128,
                                     128 * (qf - 8):128 * (qf - 8) + 128], vtile)
                    wait("tensor", tv2)
            wait("tensor", tv)
        prev_t = mark("tensor")
        tv_end = mark("vector")
        wait("scalar", prev_t)
        wait("scalar", tv_end)
    fence("vector")
    for qf in range(8):
        V(lambda e, qf=qf: e.tensor_scalar(qlkl[:, qf, :], lmacc[:, qf, :],
                                           1.0 / 16.0, None, Alu.mult))
    full_barrier()
    ckpt()

    kh_b = bf16s(0, SEQ)
    a3b = bf16s(4096, 2, SEQ)
    vhsm = bf16s(12288, NST, 64)
    zb = bf16s(14336, 2, 256)
    zcur_b = bf16s(14848, 2, 256)
    XT = bf16s(15360, 2, 256)
    wrkb = bf16s(15872, 2, 256)
    a3Tt = bf16s(16384, 2, 128)
    t2bb = bf16s(16640, 2, 64)
    scrT = bf16s(16768, 2, 256)
    att3f = f32s(0, 2, SEQ)
    zmat = f32s(8192, 2, 256)
    xzf = f32s(8704, 2, 256)
    wrkf = f32s(9216, 2, 256)

    for h in range(NHEAD):
        hcol = 64 * (h % 2)
        hch = h // 2
        qlh = qlkl[hcol:hcol + 64, hch, :]
        klh = qlkl[hcol:hcol + 64, 4 + hch, :]
        for lc in range(2):
            pb = ps[lc]
            T(lambda e, pb=pb, lc=lc, qlh=qlh, klh=klh: e.matmul(
                pb[:, 0:256], qlh[:, 128 * lc:128 * lc + 128], klh,
                start=True, stop=True))
        tmm = mark("tensor")
        wait("vector", tmm)
        for lc in range(2):
            V(lambda e, lc=lc: e.tensor_copy(zmat[:, lc, :], ps[lc][:, 0:256]))
        tv = mark("vector")
        wait("scalar", tv)
        wait("tensor", tv)
        soft_rows([zmat[:, 0, :], zmat[:, 1, :]], [zb[:, 0, :], zb[:, 1, :]])
        tv = mark("vector")
        wait("tensor", tv)
        wait("scalar", tv)
        for lc in range(2):
            V(lambda e, lc=lc: e.tensor_reduce(small[:, lc:lc + 1], zb[:, lc, :],
                                               AX.X, Alu.add))
        tv = mark("vector")
        wait("tensor", tv)
        for lc in range(2):
            pb = bank()
            T(lambda e, pb=pb, lc=lc: e.transpose(pb[0:1, 0:128], small[:, lc:lc + 1], ident))
            tmm = mark("tensor")
            wait("vector", tmm)
            V(lambda e, pb=pb, lc=lc: e.tensor_copy(srow[:, 0, 128 * lc:128 * lc + 128],
                                                    pb[0:1, 0:128]))
            tv = mark("vector")
            wait("tensor", tv)
        pb = ps[2]
        for lc in range(2):
            T(lambda e, pb=pb, lc=lc: e.matmul(pb[0:1, 0:256], onescol_b, zb[:, lc, :],
                                               start=(lc == 0), stop=(lc == 1)))
        tmm = mark("tensor")
        wait("vector", tmm)
        V(lambda e, pb=pb: e.tensor_copy(srow[:, 1, 0:256], pb[0:1, 0:256]))
        fence("vector")
        V(lambda e: e.tensor_reduce(srow[:, 2, 0:1], srow[:, 0, 0:256], AX.X, Alu.max))
        V(lambda e: e.tensor_reduce(srow[:, 2, 1:2], srow[:, 1, 0:256], AX.X, Alu.max))
        fence("vector")
        V(lambda e: e.tensor_tensor(srow[:, 2, 2:3], srow[:, 2, 0:1], srow[:, 2, 1:2],
                                    Alu.mult))
        fence("vector")
        V(lambda e: e.reciprocal(srow[:, 2, 3:4], srow[:, 2, 2:3]))
        tv = mark("vector")
        wait("tensor", tv)
        T(lambda e: e.matmul(ps[3][:, 0:1], onesrow, srow[:, 2, 3:4], start=True, stop=True))
        tmm = mark("tensor")
        wait("vector", tmm)
        V(lambda e: e.tensor_copy(small[:, 12:13], ps[3][:, 0:1]))
        tv = mark("vector")
        wait("tensor", tv)
        for ic in range(2):
            for jc in range(2):
                V(lambda e, ic=ic, jc=jc: e.tensor_copy(
                    trf, zb[:, ic, 128 * jc:128 * jc + 128]))
                tv = mark("vector")
                wait("tensor", tv)
                pb = bank()
                T(lambda e, pb=pb: e.transpose(pb[:, 0:128], trf, ident))
                tmm = mark("tensor")
                wait("scalar", tmm)
                S(lambda e, pb=pb, ic=ic, jc=jc: e.activation(
                    wrkf[:, jc, 128 * ic:128 * ic + 128], pb[:, 0:128], AF.Identity,
                    scale=small[:, 12:13]))
                ts = mark("scalar")
                wait("tensor", ts)
                wait("vector", ts)
        ts = mark("scalar")
        wait("vector", ts)
        for lc in range(2):
            V(lambda e, lc=lc: e.tensor_copy(zcur_b[:, lc, :], wrkf[:, lc, :]))
        tv = mark("vector")
        wait("tensor", tv)
        transpose256(XT, zb)
        for it in range(6):
            mm256(xzf, XT, zcur_b)
            wait("scalar", mark("vector"))
            for lc in range(2):
                S(lambda e, lc=lc: e.activation(wrkf[:, lc, :], xzf[:, lc, :],
                                                AF.Identity, scale=-1.0))
            ts = mark("scalar")
            wait("vector", ts)
            for lc in range(2):
                V(lambda e, lc=lc: e.scalar_tensor_tensor(
                    wrkf[:, lc, 128 * lc:128 * lc + 128], ident, 7.0,
                    wrkf[:, lc, 128 * lc:128 * lc + 128], Alu.mult, Alu.add))
            fence("vector")
            for lc in range(2):
                V(lambda e, lc=lc: e.tensor_copy(wrkb[:, lc, :], wrkf[:, lc, :]))
                V(lambda e, lc=lc: e.tensor_copy(zb[:, lc, :], xzf[:, lc, :]))
            tv = mark("vector")
            wait("tensor", tv)
            transpose256(scrT, zb)
            mm256(xzf, scrT, wrkb)
            wait("scalar", mark("vector"))
            for lc in range(2):
                S(lambda e, lc=lc: e.activation(wrkf[:, lc, :], xzf[:, lc, :],
                                                AF.Identity, scale=-1.0))
            ts = mark("scalar")
            wait("vector", ts)
            for lc in range(2):
                V(lambda e, lc=lc: e.scalar_tensor_tensor(
                    wrkf[:, lc, 128 * lc:128 * lc + 128], ident, 15.0,
                    wrkf[:, lc, 128 * lc:128 * lc + 128], Alu.mult, Alu.add))
            fence("vector")
            for lc in range(2):
                V(lambda e, lc=lc: e.tensor_copy(wrkb[:, lc, :], wrkf[:, lc, :]))
            tv = mark("vector")
            wait("tensor", tv)
            mm256(xzf, scrT, wrkb)
            wait("scalar", mark("vector"))
            for lc in range(2):
                S(lambda e, lc=lc: e.activation(wrkf[:, lc, :], xzf[:, lc, :],
                                                AF.Identity, scale=-1.0))
            ts = mark("scalar")
            wait("vector", ts)
            for lc in range(2):
                V(lambda e, lc=lc: e.scalar_tensor_tensor(
                    wrkf[:, lc, 128 * lc:128 * lc + 128], ident, 13.0,
                    wrkf[:, lc, 128 * lc:128 * lc + 128], Alu.mult, Alu.add))
            fence("vector")
            for lc in range(2):
                V(lambda e, lc=lc: e.tensor_copy(wrkb[:, lc, :], wrkf[:, lc, :]))
            tv = mark("vector")
            wait("tensor", tv)
            transpose256(scrT, zcur_b)
            mm256(xzf, scrT, wrkb)
            wait("scalar", mark("vector"))
            for lc in range(2):
                S(lambda e, lc=lc: e.activation(zmat[:, lc, :], xzf[:, lc, :],
                                                AF.Identity, scale=0.25))
            ts = mark("scalar")
            wait("vector", ts)
            for lc in range(2):
                V(lambda e, lc=lc: e.tensor_copy(zcur_b[:, lc, :], zmat[:, lc, :]))
            tv = mark("vector")
            wait("tensor", tv)
        a2iT = scrT
        transpose256(a2iT, zcur_b)
        t_k = ld(kh_b[hcol:hcol + 64, :], qkv_s[512 + 64 * h:576 + 64 * h, :])
        wait("tensor", t_k)
        for lc in range(2):
            for stl in range(8):
                pb = bank()
                T(lambda e, pb=pb, lc=lc, stl=stl, qlh=qlh, hcol=hcol: e.matmul(
                    pb, qlh[:, 128 * lc:128 * lc + 128],
                    kh_b[hcol:hcol + 64, SLAB * stl:SLAB * stl + SLAB],
                    start=True, stop=True))
                tmm = mark("tensor")
                wait("vector", tmm)
                V(lambda e, pb=pb, lc=lc, stl=stl: e.tensor_copy(
                    att3f[:, lc, SLAB * stl:SLAB * stl + SLAB], pb))
                tv = mark("vector")
                wait("tensor", tv)
        soft_rows([att3f[:, 0, :], att3f[:, 1, :]], [a3b[:, 0, :], a3b[:, 1, :]])
        tv = mark("vector")
        wait("tensor", tv)
        wait("sync", tv)
        t_v = ld(vhsm, vsm_s[:, 64 * h:64 * h + 64].rearrange("(a p) f -> p a f", p=128))
        wait("tensor", t_v)
        for sc in range(NST):
            for lc in range(2):
                tr128(a3Tt[:, lc, :], a3b[:, lc, 128 * sc:128 * sc + 128])
            for lc in range(2):
                T(lambda e, sc=sc, lc=lc: e.matmul(
                    ps[4 + lc][:, 0:64], a3Tt[:, lc, :], vhsm[:, sc, :],
                    start=(sc == 0), stop=(sc == NST - 1)))
        tmm = mark("tensor")
        wait("vector", tmm)
        for lc in range(2):
            V(lambda e, lc=lc: e.tensor_copy(t2bb[:, lc, :], ps[4 + lc][:, 0:64]))
        tv = mark("vector")
        wait("tensor", tv)
        for lc in range(2):
            pb = bank()
            for kc2 in range(2):
                T(lambda e, pb=pb, lc=lc, kc2=kc2: e.matmul(
                    pb[:, 0:64], a2iT[:, kc2, 128 * lc:128 * lc + 128], t2bb[:, kc2, :],
                    start=(kc2 == 0), stop=(kc2 == 1)))
            tmm = mark("tensor")
            wait("vector", tmm)
            V(lambda e, pb=pb, h=h, lc=lc: e.tensor_copy(t3b[:, h, lc, :], pb[:, 0:64]))
            tv = mark("vector")
            wait("tensor", tv)
        full_barrier()
    ckpt()

    qgrp = bf16s(0, 4, SLAB)
    vgrp = bf16s(2048, 6, SLAB)
    a1all = bf16s(5120, 8, 256)
    a1T = bf16s(7168, 2, 128)
    outwb = bf16s(7424, 4, F)
    outrB = bf16s(11520, SLAB)
    outrT = bf16s(12032, 4, 128)
    a1f = f32s(0, 8, 256)
    xg0cols = f32s(2048, KC, 128)
    outrF = f32s(3072, SLAB)
    wstage3 = f32s(4096, 4, F)
    t_w = ld(wstage3, outw_d.rearrange("(a p) f -> p a f", p=128))
    wait("vector", t_w)
    for a in range(4):
        V(lambda e, a=a: e.tensor_copy(outwb[:, a, :], wstage3[:, a, :]))
    full_barrier()

    for sg in range(8):
        c0 = SLAB * sg
        lo = c0 - 128
        t_q = ld(qgrp, qkv_s[0:512, c0:c0 + SLAB].rearrange("(a p) s -> p a s", p=128))
        V(lambda e: e.memset(vgrp.rearrange("p a b -> p (a b)"), 0.0))
        tv0 = mark("vector")
        wait("sync", tv0)
        for ch in range(6):
            r0 = lo + 128 * ch
            if r0 < 0 or r0 >= SEQ:
                continue
            ld(vgrp[:, ch, :], vsm_s[r0:r0 + 128, :])
        t_vl = (s_ld, p.val(s_ld))
        wait("tensor", t_q)
        wait("tensor", t_vl)
        for sc4 in range(4):
            scg = 4 * sg + sc4
            pbout = ps[6]
            for h in range(NHEAD):
                hcol = 64 * (h % 2)
                hch = h // 2
                pb = bank()
                T(lambda e, pb=pb, h=h, hcol=hcol, hch=hch, sc4=sc4: e.matmul(
                    pb[:, 0:256], qgrp[hcol:hcol + 64, hch, 128 * sc4:128 * sc4 + 128],
                    qlkl[hcol:hcol + 64, 4 + hch, :], start=True, stop=True))
                tmm = mark("tensor")
                wait("vector", tmm)
                V(lambda e, pb=pb, h=h: e.tensor_copy(a1f[:, h, :], pb[:, 0:256]))
                tv = mark("vector")
                wait("tensor", tv)
            soft_rows([a1f[:, h, :] for h in range(NHEAD)],
                      [a1all[:, h, :] for h in range(NHEAD)])
            tv = mark("vector")
            wait("tensor", tv)
            group = []
            for h in range(NHEAD):
                for lc in range(2):
                    group.append(("a1", h, lc))
                for dd in range(3):
                    ch = sc4 + dd
                    r0 = lo + 128 * ch
                    if r0 < 0 or r0 >= SEQ:
                        continue
                    group.append(("cv", h, dd, ch))
            ng = len(group)
            gi = [0]
            for h in range(NHEAD):
                for lc in range(2):
                    tr128(a1T[:, lc, :], a1all[:, h, 128 * lc:128 * lc + 128])
                    i = gi[0]
                    gi[0] += 1
                    T(lambda e, h=h, lc=lc, pbout=pbout, st=(i == 0), sp=(i == ng - 1):
                      e.matmul(pbout[:, 64 * h:64 * h + 64], a1T[:, lc, :],
                               t3b[:, h, lc, :], start=st, stop=sp))
                for dd in range(3):
                    ch = sc4 + dd
                    r0 = lo + 128 * ch
                    if r0 < 0 or r0 >= SEQ:
                        continue
                    i = gi[0]
                    gi[0] += 1
                    T(lambda e, h=h, dd=dd, ch=ch, pbout=pbout, st=(i == 0), sp=(i == ng - 1):
                      e.matmul(pbout[:, 64 * h:64 * h + 64], convT[:, h, dd, :],
                               vgrp[:, ch, 64 * h:64 * h + 64], start=st, stop=sp))
            tmm = mark("tensor")
            wait("vector", tmm)
            V(lambda e, pbout=pbout: e.tensor_copy(outrF, pbout))
            tv = mark("vector")
            wait("tensor", tv)
            for fd in range(4):
                pb = bank()
                T(lambda e, pb=pb, fd=fd: e.transpose(
                    pb[:, 0:128], outrF[:, 128 * fd:128 * fd + 128], ident))
                tmm = mark("tensor")
                wait("vector", tmm)
                V(lambda e, pb=pb, fd=fd: e.tensor_copy(outrT[:, fd, :], pb[:, 0:128]))
                tv = mark("vector")
                wait("tensor", tv)
            for f2 in range(KC):
                pb = bank()
                for fd in range(4):
                    T(lambda e, pb=pb, f2=f2, fd=fd: e.matmul(
                        pb[:, 0:128], outwb[:, fd, 128 * f2:128 * f2 + 128],
                        outrT[:, fd, :], start=(fd == 0), stop=(fd == 3)))
                tmm = mark("tensor")
                wait("scalar", tmm)
                S(lambda e, pb=pb, f2=f2: e.activation(
                    xg0cols[:, f2, :], pb[:, 0:128], AF.Identity,
                    bias=outb[:, f2:f2 + 1]))
                ts = mark("scalar")
                wait("tensor", ts)
            ts = mark("scalar")
            wait("gpsimd", ts)
            stw(fmv(xg0_s)[:, :, 128 * scg:128 * scg + 128], xg0cols)
            wait("scalar", (s_sp, p.val(s_sp)))
        full_barrier()
    ckpt()

    q0f = f32s(0, 2, SEQ)
    k0f = f32s(8192, 2, SEQ)
    wstage4 = f32s(16384, KC, 256)
    wqb = bf16s(8192, KC, 256)
    wkb = bf16s(10240, KC, 256)
    enc0b = bf16s(0, KC, 1024)
    t_w = ld(wstage4, fmv(wq_d))
    wait("vector", t_w)
    for k in range(KC):
        V(lambda e, k=k: e.tensor_copy(wqb[:, k, :], wstage4[:, k, :]))
    tvw = mark("vector")
    wait("sync", tvw)
    t_w = ld(wstage4, fmv(wk_d))
    wait("vector", t_w)
    for k in range(KC):
        V(lambda e, k=k: e.tensor_copy(wkb[:, k, :], wstage4[:, k, :]))
    full_barrier()

    xga = f32s(16384, 1024)
    dfa = f32s(17408, 1024)
    for qt in range(4):
        s0 = 1024 * qt
        for k in range(KC):
            t_a = ld(xga, fmv(xg0_s)[:, k, s0:s0 + 1024])
            t_b = ld(dfa, fmv(d0f_s)[:, k, s0:s0 + 1024])
            wait("vector", t_a)
            wait("vector", t_b)
            V(lambda e, k=k: e.tensor_tensor(enc0b[:, k, :], xga, dfa, Alu.add))
            tv = mark("vector")
            wait("sync", tv)
        tv = mark("vector")
        wait("tensor", tv)
        for qf in range(2):
            for stl in range(2):
                pb = bank()
                for k in range(KC):
                    T(lambda e, pb=pb, qf=qf, stl=stl, k=k: e.matmul(
                        pb, wqb[:, k, 128 * qf:128 * qf + 128],
                        enc0b[:, k, SLAB * stl:SLAB * stl + SLAB],
                        start=(k == 0), stop=(k == KC - 1)))
                tmm = mark("tensor")
                wait("scalar", tmm)
                S(lambda e, pb=pb, qf=qf, stl=stl, s0=s0: e.activation(
                    q0f[:, qf, s0 + SLAB * stl:s0 + SLAB * stl + SLAB], pb,
                    AF.Identity, scale=0.0625))
                ts = mark("scalar")
                wait("tensor", ts)
                pb = bank()
                for k in range(KC):
                    T(lambda e, pb=pb, qf=qf, stl=stl, k=k: e.matmul(
                        pb, wkb[:, k, 128 * qf:128 * qf + 128],
                        enc0b[:, k, SLAB * stl:SLAB * stl + SLAB],
                        start=(k == 0), stop=(k == KC - 1)))
                tmm = mark("tensor")
                wait("scalar", tmm)
                S(lambda e, pb=pb, qf=qf, stl=stl, s0=s0: e.activation(
                    k0f[:, qf, s0 + SLAB * stl:s0 + SLAB * stl + SLAB], pb, AF.Identity))
                ts = mark("scalar")
                wait("tensor", ts)
        wait("vector", mark("scalar"))
        tv = mark("vector")
        wait("sync", tv)
    q0b = bf16s(0, 2, SEQ)
    k0b = bf16s(8192, 2, SEQ)
    wait("vector", mark("scalar"))
    for qf in range(2):
        V(lambda e, qf=qf: e.tensor_copy(q0b[:, qf, :], q0f[:, qf, :]))
        V(lambda e, qf=qf: e.tensor_copy(k0b[:, qf, :], k0f[:, qf, :]))
    full_barrier()
    ckpt()

    Dst = f32s(0, SEQ)
    DS = f32s(4096, SEQ)
    Sst = f32s(8192, SEQ)
    clampvt = f32s(12288, NST * C1)
    t_cl = ld(clampvt, clampv_d)
    wait("vector", t_cl)
    for st in range(NST):
        t_s = ld(Sst, Sd[128 * st:128 * st + 128, :])
        for ct in range(8):
            pb = bank()
            for qf in range(2):
                T(lambda e, pb=pb, st=st, ct=ct, qf=qf: e.matmul(
                    pb, q0b[:, qf, 128 * st:128 * st + 128],
                    k0b[:, qf, SLAB * ct:SLAB * ct + SLAB],
                    start=(qf == 0), stop=(qf == 1)))
            tmm = mark("tensor")
            wait("vector", tmm)
            V(lambda e, pb=pb, ct=ct: e.tensor_copy(Dst[:, SLAB * ct:SLAB * ct + SLAB], pb))
            tv = mark("vector")
            wait("tensor", tv)
        wait("vector", t_s)
        V(lambda e: e.tensor_tensor(DS, Dst, Sst, Alu.mult))
        V(lambda e, st=st: e.tensor_reduce(araw[:, st:st + 1], DS, AX.X, Alu.add))
        V(lambda e, st=st: e.tensor_reduce(
            small[:, 0:1], clampvt.rearrange("p (a b) -> p a b", a=NST)[:, st, :],
            AX.X, Alu.add))
        fence("vector")
        V(lambda e: e.tensor_tensor(small[:, 1:2], small[:, 0:1],
                                    Dst[:, SEQ - 1:SEQ], Alu.mult))
        fence("vector")
        V(lambda e, st=st: e.tensor_tensor(araw[:, st:st + 1], araw[:, st:st + 1],
                                           small[:, 1:2], Alu.add))
        tv = mark("vector")
        wait("sync", tv)
        wait("tensor", tv)
    full_barrier()

    EDG = SEQ * C2
    dupvt = f32s(0, EDG)
    prodf = f32s(EDG, EDG)
    arowf = f32s(2 * EDG, SEQ)
    arept = f32s(2 * EDG + SEQ, NST)
    t_dv = ld(dupvt[0:1, :], dupv_d)
    k0g = f32s(8192, 2, SEQ)
    for qf in range(2):
        V(lambda e, qf=qf: e.tensor_copy(k0g[:, qf, :], k0b[:, qf, :]))
    kgt = f32s(16384, 2, SLAB)
    mult = f32s(17408, SLAB)
    tvk = mark("vector")
    wait("gpsimd", tvk)
    p.on("gpsimd", lambda e: e.load_library(_APG_LIB), kind="dma")
    ntile = EDG // SLAB
    for tl in range(ntile):
        a0 = SLAB * tl
        idxsl = dupct[:, (a0 // 16):(a0 // 16) + SLAB // 16]
        for qf in range(2):
            G(lambda e, qf=qf, idxsl=idxsl: e.ap_gather(
                kgt[:, qf, :], k0g[:, qf, :], idxsl,
                channels=128, num_elems=SEQ, d=1, num_idxs=SLAB))
        tg = mark("gpsimd")
        wait("vector", tg)
        q0bc = [q0b[:, qf, :].rearrange("p (s o) -> p s o", o=1)
                .broadcast_to([128, SEQ, C2]).rearrange("p s c -> p (s c)")[:, a0:a0 + SLAB]
                for qf in range(2)]
        pb = ps[4]
        for qf in range(2):
            V(lambda e, qf=qf, q0bc=q0bc: e.tensor_tensor(mult, kgt[:, qf, :],
                                                          q0bc[qf], Alu.mult))
            tv = mark("vector")
            wait("tensor", tv)
            T(lambda e, pb=pb, qf=qf: e.matmul(pb[0:1, :], onescol, mult,
                                               start=(qf == 0), stop=(qf == 1)))
            tmm = mark("tensor")
            if qf == 0:
                wait("vector", tmm)
        tmm = mark("tensor")
        wait("vector", tmm)
        V(lambda e, pb=pb, a0=a0: e.tensor_copy(prodf[0:1, a0:a0 + SLAB], pb[0:1, :]))
        tv = mark("vector")
        wait("gpsimd", tv)
    p.on("gpsimd", lambda e: e.load_library(_STD_LIB), kind="dma")
    tg = mark("gpsimd")
    wait("vector", tg)
    wait("vector", t_dv)
    V(lambda e: e.tensor_tensor(prodf[0:1, :], prodf[0:1, :], dupvt[0:1, :], Alu.mult))
    V(lambda e: e.tensor_reduce(arowf[0:1, :],
                                prodf[0:1, :].rearrange("p (s c) -> p s c", c=C2),
                                AX.X, Alu.add))
    tv = mark("vector")
    wait("gpsimd", tv)
    t_w1 = stw(asc_s, arowf[0:1, :])
    wait("sync", t_w1)
    t_r = ld(arept, asc_s.rearrange("o (a p) -> (o p) a", p=128), slow=True)
    wait("vector", t_r)
    V(lambda e: e.tensor_tensor(araw, araw, arept, Alu.add))
    fence("vector")
    V(lambda e: e.tensor_reduce(small[:, 0:1], araw, AX.X, Alu.max))
    tv = mark("vector")
    wait("tensor", tv)
    pb = bank()
    T(lambda e, pb=pb: e.transpose(pb[0:1, 0:128], small[:, 0:1], ident))
    tmm = mark("tensor")
    wait("vector", tmm)
    V(lambda e, pb=pb: e.tensor_copy(srow[:, 0, 0:128], pb[0:1, 0:128]))
    fence("vector")
    V(lambda e: e.tensor_reduce(srow[:, 0, 128:129], srow[:, 0, 0:128], AX.X, Alu.max))
    tv = mark("vector")
    wait("tensor", tv)
    T(lambda e: e.matmul(ps[4][:, 0:1], onesrow, srow[:, 0, 128:129], start=True, stop=True))
    tmm = mark("tensor")
    wait("scalar", tmm)
    S(lambda e: e.activation(small[:, 2:3], ps[4][:, 0:1], AF.Identity, scale=-1.0))
    ts = mark("scalar")
    wait("scalar", ts)
    S(lambda e: e.activation(araw, araw, AF.Exp, bias=small[:, 2:3]))
    ts = mark("scalar")
    wait("vector", ts)
    V(lambda e: e.tensor_reduce(small[:, 3:4], araw, AX.X, Alu.add))
    tv = mark("vector")
    wait("tensor", tv)
    pb = bank()
    T(lambda e, pb=pb: e.transpose(pb[0:1, 0:128], small[:, 3:4], ident))
    tmm = mark("tensor")
    wait("vector", tmm)
    V(lambda e, pb=pb: e.tensor_copy(srow[:, 0, 0:128], pb[0:1, 0:128]))
    fence("vector")
    V(lambda e: e.tensor_reduce(srow[:, 0, 129:130], srow[:, 0, 0:128], AX.X, Alu.add))
    fence("vector")
    V(lambda e: e.reciprocal(srow[:, 0, 130:131], srow[:, 0, 129:130]))
    tv = mark("vector")
    wait("tensor", tv)
    T(lambda e: e.matmul(ps[5][:, 0:1], onesrow, srow[:, 0, 130:131], start=True, stop=True))
    tmm = mark("tensor")
    wait("vector", tmm)
    V(lambda e: e.tensor_scalar(araw, araw, ps[5][:, 0:1], None, Alu.mult))
    tv = mark("vector")
    wait("gpsimd", tv)
    t_w2 = stw(asc_s.rearrange("o (a p) -> (o p) a", p=128), araw, slow=True)
    wait("sync", t_w2)
    t_r = ld(arowf[0:1, :], asc_s)
    alpha_full = f32s(0, SEQ)
    wait("tensor", t_r)
    for tl in range(8):
        T(lambda e, tl=tl: e.matmul(ps[4], onesrow, arowf[0:1, SLAB * tl:SLAB * tl + SLAB],
                                    start=True, stop=True))
        tmm = mark("tensor")
        wait("vector", tmm)
        V(lambda e, tl=tl: e.tensor_copy(alpha_full[:, SLAB * tl:SLAB * tl + SLAB], ps[4]))
        tv = mark("vector")
        wait("tensor", tv)
    tv = mark("vector")
    wait("gpsimd", tv)
    p.on("gpsimd", lambda e: e.load_library(_APG_LIB), kind="dma")
    G(lambda e: e.ap_gather(alphaw, alpha_full, winmap, channels=128,
                            num_elems=SEQ, d=1, num_idxs=WIN))
    xg0stage = f32s(4096, SEQ)
    for k in range(KC):
        t_x = ld(xg0stage, fmv(xg0_s)[:, k, :])
        wait("gpsimd", t_x)
        G(lambda e, k=k: e.ap_gather(xg0w[:, k, :], xg0stage, winmap, channels=128,
                                     num_elems=SEQ, d=1, num_idxs=WIN))
        tg = mark("gpsimd")
        wait("sync", tg)
    fence("gpsimd")
    p.on("gpsimd", lambda e: e.load_library(_STD_LIB), kind="dma")
    full_barrier()
    ckpt()

    wvt = bf16s(0, KC, F)
    xbsl2 = bf16s(8192, KC, SLAB)
    ybsl2 = bf16s(8192, KC, SLAB)
    sqsl2 = bf16s(12288, KC, SLAB)
    dbsl = bf16s(12288, KC, SLAB)
    msl2 = bf16s(12288, KC, SLAB)
    hbuf2 = bf16s(16384, SLAB)
    x1sl = f32s(0, KC, SLAB)
    dsl = f32s(4096, KC, SLAB)
    valf = f32s(8192, KC, SLAB)
    x2sl = f32s(12288, KC, SLAB)
    tmpa = f32s(16384, SLAB)
    tmpb = f32s(16896, SLAB)
    tmpc = f32s(17408, SLAB)
    xq2 = fp8s(0, KC, SLAB)
    hq2 = fp8s(4096, HC, SLAB)

    wstage5 = f32s(0, KC, F)
    t_w = ld(wstage5, fmv(wv_d))
    wait("vector", t_w)
    for k in range(KC):
        V(lambda e, k=k: e.tensor_copy(wvt[:, k, :], wstage5[:, k, :]))
    full_barrier()

    for sl in range(8):
        c0 = SLAB * sl
        t_x1 = ld(x1sl, fmv(x1s)[:, :, SEQ + c0:SEQ + c0 + SLAB])
        wait("vector", t_x1)
        wait("scalar", t_x1)
        emit_ln(x1sl, xbsl2, sqsl2, tmpa, out_f32=dsl, out_bf=dbsl)
        tv = mark("vector")
        wait("tensor", tv)
        for fo in range(KC):
            pb = bank()
            for k in range(KC):
                T(lambda e, pb=pb, fo=fo, k=k: e.matmul(
                    pb, wvt[:, k, 128 * fo:128 * fo + 128], dbsl[:, k, :],
                    start=(k == 0), stop=(k == KC - 1)))
            tmm = mark("tensor")
            wait("scalar", tmm)
            S(lambda e, pb=pb, fo=fo: e.activation(valf[:, fo, :], pb, AF.Identity,
                                                   bias=wvb[:, fo:fo + 1]))
            ts = mark("scalar")
            wait("tensor", ts)
        t_val = mark("scalar")
        wait("vector", t_val)
        for k in range(KC):
            V(lambda e, k=k: e.tensor_tensor(tmpa, valf[:, k, :], alphaw, Alu.mult))
            tv1 = mark("vector")
            wait("scalar", tv1)
            S(lambda e: e.activation(tmpb, tmpa, AF.Sigmoid, scale=-1.0))
            S(lambda e: e.activation(tmpb, tmpb, AF.Square))
            ts1 = mark("scalar")
            wait("vector", ts1)
            V(lambda e: e.tensor_scalar(tmpb, tmpb, 2.0, None, Alu.mult))
            V(lambda e, k=k: e.tensor_tensor(tmpc, xg0w[:, k, :], dsl[:, k, :], Alu.add))
            V(lambda e: e.tensor_tensor(tmpa, tmpa, tmpb, Alu.mult))
            V(lambda e: e.tensor_scalar(tmpb, tmpb, -1.0, 2.0, Alu.mult, Alu.add))
            V(lambda e: e.tensor_tensor(tmpc, tmpc, tmpb, Alu.mult))
            V(lambda e: e.tensor_tensor(tmpa, tmpa, tmpc, Alu.add))
            V(lambda e, k=k: e.tensor_tensor(x2sl[:, k, :], x1sl[:, k, :], tmpa, Alu.add))
        tv = mark("vector")
        wait("scalar", tv)
        emit_ln(x2sl, xbsl2, sqsl2, tmpa, out_f8=xq2)
        tv = mark("vector")
        wait("tensor", tv)
        wait("sync", tv)
        t_m = ld(msl2, fmv(mask_own)[:, :, c0:c0 + SLAB])
        emit_ffn(xq2, hbuf2, hq2, ybsl2)
        wait("vector", mark("scalar"))
        wait("vector", t_m)
        for k in range(KC):
            V(lambda e, k=k: e.tensor_tensor(tmpa, ybsl2[:, k, :], msl2[:, k, :], Alu.mult))
            V(lambda e, k=k: e.tensor_tensor(x2sl[:, k, :], x2sl[:, k, :], tmpa, Alu.add))
        tv = mark("vector")
        wait("gpsimd", tv)
        stw(fmv(yout)[:, :, c0:c0 + SLAB], x2sl)
        full_barrier()

    if upto is not None:
        lens, cnts, atts = _ckpts[upto]
        for e in p.ops:
            del p.ops[e][lens[e]:]
            del p.kinds[e][lens[e]:]
            p.attach[e] = {i: v[:atts[e][i]] for i, v in p.attach[e].items()
                           if i in atts[e]}
        p.cnt.clear()
        p.cnt.update(cnts)
        zsrc = f32s(0, 1, NT)
        for k in range(KC):
            stw(fmv(yout)[:, k:k + 1, :], zsrc)
        svv = p.val(s_sp)
        for e in ALL:
            p.on(e, lambda en, svv=svv: en.wait_ge(s_sp, svv), kind="wait")

    lv, sv = p.val(s_ld), p.val(s_sp)
    p.on("gpsimd", lambda e: e.wait_ge(s_sp, sv), kind="wait")
    p.on("sync", lambda e: e.wait_ge(s_ld, lv), kind="wait")

    with nc.Block() as block:
        @block.sync
        def _(sync):
            p.replay("sync", sync)

        @block.gpsimd
        def _(g):
            p.replay("gpsimd", g)

        @block.scalar
        def _(s):
            p.replay("scalar", s)

        @block.vector
        def _(v):
            p.replay("vector", v)

        @block.tensor
        def _(t):
            p.replay("tensor", t)

    for cm in reversed(_sem_keep):
        cm.__exit__(None, None, None)
    mybir.codegen_inst_isa_subclasses(nc)
    return nc


def kernel(L0_attn_dense, L0_attn_adj_rows, L0_attn_adj_cols, L0_attn_adj_vals,
           L0_attn_wq, L0_attn_wk, L0_attn_wv_w, L0_attn_wv_b, L0_attn_qkv_w,
           L0_attn_out_w, L0_attn_out_b, L0_attn_conv_w,
           L0_ffn_W1, L0_ffn_b1, L0_ffn_W2, L0_ffn_b2):
    x = np.ascontiguousarray(np.asarray(L0_attn_dense, np.float32))
    S, clampv, C1, dupc_flat, dupv, C2 = _prep_sparse(
        L0_attn_adj_rows, L0_attn_adj_cols, L0_attn_adj_vals)
    mask = _dropout_mask()
    maskv = np.where(mask, np.float32(0.625), np.float32(0.0)).astype(NPBF16)

    nc = build_nc(C1, C2)

    common = {
        "W1": np.ascontiguousarray(np.asarray(L0_ffn_W1, np.float32)),
        "W2": np.ascontiguousarray(np.asarray(L0_ffn_W2, np.float32)),
        "b1v": _fold_bias(L0_ffn_b1, HC),
        "b2v": _fold_bias(L0_ffn_b2, KC),
        "wvb": _fold_bias(L0_attn_wv_b, KC),
        "outb": _fold_bias(L0_attn_out_b, KC),
        "wq": np.ascontiguousarray(np.asarray(L0_attn_wq, np.float32)),
        "wk": np.ascontiguousarray(np.asarray(L0_attn_wk, np.float32)),
        "wv": np.ascontiguousarray(np.asarray(L0_attn_wv_w, np.float32)),
        "qkvw": np.ascontiguousarray(np.asarray(L0_attn_qkv_w, np.float32)),
        "outw": np.ascontiguousarray(np.asarray(L0_attn_out_w, np.float32)),
        "convT": _conv_toeplitz(L0_attn_conv_w),
        "Sd": S,
        "clampv": clampv,
        "dupc": _interleave_idx(dupc_flat),
        "dupv": dupv,
        "xb0": _fm(x[0]),
        "mask_b0": np.ascontiguousarray(maskv[0].T),
        "ident": np.eye(128, dtype=np.float32),
        "onesrow": np.ones((1, 128), np.float32),
        "onescol": np.ones((128, 1), np.float32),
    }
    in_maps = []
    for c in range(8):
        w0, w1 = WIN * c, WIN * c + WIN
        m = dict(common)
        m["xown"] = np.ascontiguousarray(x[:, w0:w1, :].reshape(8 * WIN, F).T)
        m["mask_own"] = np.ascontiguousarray(
            maskv[:, w0:w1, :].reshape(8 * WIN, F).T)
        m["winmap"] = _interleave_idx(np.arange(w0, w1))
        in_maps.append(m)

    res = run_bass_kernel_spmd(nc, in_maps, core_ids=list(range(8)))
    out = np.empty((8, SEQ, F), np.float32)
    for c in range(8):
        yo = np.asarray(res.results[c]["yout"])
        out[:, WIN * c:WIN * c + WIN, :] = yo.T.reshape(8, WIN, F)
    return out
